# version 01
# baseline (speedup 1.0000x reference)
import contextlib
import numpy as np
import concourse.bass as bass
import concourse.mybir as mybir
from concourse.bass_utils import run_bass_kernel_spmd

F32 = mybir.dt.float32
F16 = mybir.dt.float16
I32 = mybir.dt.int32
AF = mybir.ActivationFunctionType
ALU = mybir.AluOpType
AX = mybir.AxisListType

P = 128
T = 2048
D = 2048
KCD = D // P
NTT = 4
TTW = 512
INTER = 5632
NIC = INTER // P
NH = 16
HD = 128
GRP = 4
NG = NH // GRP
QSCALE = float(1.0 / np.sqrt(128.0))
BIGNEG = -60000.0
BTW = 2432
EPS = 1e-5
ROPE_BASE = 10000.0



def _blockT(WT):
    K, M = WT.shape
    kc, mc = K // P, M // P
    return np.ascontiguousarray(
        WT.reshape(kc, P, mc, P).transpose(2, 1, 0, 3).reshape(mc, P, kc * P))


def _block_grouped(WT):
    Fdim, M = WT.shape
    mc = M // P
    return np.ascontiguousarray(
        WT.reshape(NG, GRP, P, mc, P).transpose(0, 3, 2, 1, 4).reshape(NG, mc, P, GRP * P))


class _Flat:
    def __init__(self, np_dtype):
        self.parts = []
        self.off = 0
        self.dtype = np_dtype

    def add(self, arr):
        a = arr.reshape(-1, P, arr.shape[-1])
        offs = []
        for s in a:
            self.parts.append(np.ascontiguousarray(s, dtype=self.dtype))
            offs.append(self.off)
            self.off += s.shape[-1]
        return offs

    def finalize(self):
        if not self.parts:
            return np.zeros((P, 1), dtype=self.dtype)
        return np.ascontiguousarray(np.concatenate(self.parts, axis=1))


def _host_consts():
    half = HD // 2
    theta = 1.0 / ROPE_BASE ** (np.arange(half, dtype=np.float64) / half)
    m = np.arange(T, dtype=np.float64)
    fr = np.outer(m, theta)
    cos = np.cos(fr).T.astype(np.float32)
    sin = np.sin(fr).T.astype(np.float32)
    cos2 = np.concatenate([cos, cos], axis=0)
    sin2 = np.concatenate([sin, -sin], axis=0)
    pswap = np.zeros((P, P), dtype=np.float32)
    pswap[np.arange(P), (np.arange(P) + 64) % P] = 1.0
    pp = np.arange(P)[:, None]
    jj = np.arange(BTW)[None, :]
    dist = (pp - jj + 384).astype(np.float32)
    bt0 = np.where(dist <= 0, dist, BIGNEG).astype(np.float32)
    bt1 = np.where(dist <= 0, 0.0, BIGNEG).astype(np.float32)
    ident = np.eye(P, dtype=np.float32)
    cols = [ident, cos2, sin2, bt0, bt1, pswap]
    offs = np.cumsum([0] + [c.shape[1] for c in cols])
    return np.ascontiguousarray(np.concatenate(cols, axis=1)), {
        "ident": (int(offs[0]), P),
        "cos2": (int(offs[1]), T),
        "sin2": (int(offs[2]), T),
        "bt0": (int(offs[3]), BTW),
        "bt1": (int(offs[4]), BTW),
        "pswap": (int(offs[5]), P),
    }


def prepare(inputs):
    wf = _Flat(np.float32)
    wi = _Flat(np.int32)
    reg = {}
    reg["l0_wq"] = wf.add(_blockT(np.asarray(inputs["L0_attn_Wq"])))
    reg["l0_wk"] = wf.add(_blockT(np.asarray(inputs["L0_attn_Wk"])))
    reg["l0_wv"] = wf.add(_blockT(np.asarray(inputs["L0_attn_Wv"])))
    reg["l0_wo"] = wf.add(_block_grouped(np.asarray(inputs["L0_attn_Wo"])))
    reg["l0_gate"] = wf.add(_blockT(np.asarray(inputs["L0_ffn_gate_w"]).T))
    reg["l0_up"] = wf.add(_blockT(np.asarray(inputs["L0_ffn_up_w"]).T))
    reg["l0_down"] = wf.add(_blockT(np.asarray(inputs["L0_ffn_down_w"]).T))
    for nm, key in (("l1_wq", "L1_attn_W_Q"), ("l1_wk", "L1_attn_W_K"),
                    ("l1_wv", "L1_attn_W_V")):
        W = np.asarray(inputs[key])
        WT = W.transpose(2, 0, 1).reshape(D, NH * HD)
        reg[nm] = wf.add(_blockT(WT))
    reg["l1_wo"] = wf.add(_block_grouped(np.asarray(inputs["L1_attn_W_O"]).T))
    reg["l1_gate"] = wi.add(_blockT(np.asarray(inputs["L1_ffn_wq_gate"]).T))
    reg["l1_up"] = wi.add(_blockT(np.asarray(inputs["L1_ffn_wq_up"]).T))
    reg["l1_down"] = wi.add(_blockT(np.asarray(inputs["L1_ffn_wq_down"]).T))
    sv_cols = []
    sv_reg = {}

    def add_scale(name, vec):
        v = np.asarray(vec, dtype=np.float32).reshape(-1, P).T
        sv_reg[name] = (sum(c.shape[1] for c in sv_cols), v.shape[1])
        sv_cols.append(np.ascontiguousarray(v))

    add_scale("l0_gs", inputs["L0_ffn_gate_s"])
    add_scale("l0_us", inputs["L0_ffn_up_s"])
    add_scale("l0_ds", inputs["L0_ffn_down_s"])
    add_scale("l1_gs", inputs["L1_ffn_s_gate"])
    add_scale("l1_us", inputs["L1_ffn_s_up"])
    add_scale("l1_ds", inputs["L1_ffn_s_down"])
    sv = np.ascontiguousarray(np.concatenate(sv_cols, axis=1))
    cst, cst_reg = _host_consts()
    return dict(wf=wf.finalize(), wi=wi.finalize(), sv=sv, cst=cst,
                reg=reg, sv_reg=sv_reg, cst_reg=cst_reg)



class Sem:
    __slots__ = ("h", "n")

    def __init__(self, h):
        self.h = h
        self.n = 0


class B:
    ENGS = ("tensor", "vector", "scalar", "gpsimd", "sync")

    def __init__(self, nc, stack):
        self.nc = nc
        self.block = None
        self.uid = 0
        self.q = {e: [] for e in self.ENGS}
        self.LD = Sem(stack.enter_context(nc.semaphore("LD")))
        self.ST = Sem(stack.enter_context(nc.semaphore("ST")))
        self.PT = Sem(stack.enter_context(nc.semaphore("PT")))
        self.PV = Sem(stack.enter_context(nc.semaphore("PV")))
        self.PS = Sem(stack.enter_context(nc.semaphore("PS")))
        self.dbg = False
        self.tapnames = []

    def sbuf(self, stack, name, shape, dtype):
        self.uid += 1
        return stack.enter_context(
            self.nc.sbuf_tensor(f"{name}_{self.uid}", shape, dtype))

    def psum(self, stack, name, shape, dtype):
        self.uid += 1
        return stack.enter_context(
            self.nc.psum_tensor(f"{name}_{self.uid}", shape, dtype))

    def emit(self, eng, fn, waits=(), incs=()):
        for s, v in waits:
            if v > 0:
                self.q[eng].append(lambda e, s=s, v=v: e.wait_ge(s.h, v))

        def _run(e, fn=fn, incs=tuple(incs)):
            ins = fn(e)
            for s, k in incs:
                ins = ins.then_inc(s.h, k)
        self.q[eng].append(_run)
        for s, k in incs:
            s.n += k

    def tap(self, name, ap):
        if not getattr(self, "dbg", False):
            return
        shp = [int(x) for x in ap.shape]
        d = self.nc.dram_tensor(f"dbg_{name}", shp, F32, kind="ExternalOutput").ap()
        self.dma(d, ap, waits=[(self.PT, self.PT.n), (self.PV, self.PV.n),
                               (self.PS, self.PS.n)], store=True)
        self.tapnames.append(f"dbg_{name}")

    def dma(self, out, in_, waits=(), accum=False, store=False):
        sem = self.ST if store else self.LD

        def f(e, out=out, in_=in_, accum=accum):
            if accum:
                return e.dma_start(out=out, in_=in_, accum_op=ALU.add)
            return e.dma_start(out=out, in_=in_)
        self.emit("gpsimd", f, waits=waits, incs=[(sem, 16)])
        return sem.n

    def flush(self):
        for name in self.ENGS:
            ops = self.q[name]
            if ops:
                self.q[name] = []
                getattr(self.block, name)(lambda e, ops=ops: [op(e) for op in ops])


class WStream:
    def __init__(self, b, stack, name, src, offs, width, dtype=F16):
        self.b = b
        self.buf = b.sbuf(stack, name, [P, 2 * width], dtype)
        self.src = src
        self.offs = offs
        self.width = width
        self.marks = {}
        self.n = len(offs)

    def start(self):
        w0 = [(self.b.PT, self.b.PT.n)]
        for j in range(min(2, self.n)):
            self.marks[j] = self.b.dma(
                self.buf[:, (j % 2) * self.width:(j % 2 + 1) * self.width],
                self.src[:, self.offs[j]:self.offs[j] + self.width], waits=w0)
            w0 = ()

    def use(self, j):
        w = self.width
        return self.buf[:, (j % 2) * w:(j % 2 + 1) * w], self.marks[j]

    def done(self, j):
        if j + 2 < self.n:
            self.marks[j + 2] = self.b.dma(
                self.buf[:, ((j + 2) % 2) * self.width:((j + 2) % 2 + 1) * self.width],
                self.src[:, self.offs[j + 2]:self.offs[j + 2] + self.width],
                waits=[(self.b.PT, self.b.PT.n)])


def build_program(hp, layers=(0, 1), dbg=False):
    nc = bass.Bass("TRN2", target_bir_lowering=False)
    reg, sv_reg, cst_reg = hp["reg"], hp["sv_reg"], hp["cst_reg"]

    x_t = nc.dram_tensor("x_t", [D, T], F32, kind="ExternalInput").ap()
    wf = nc.dram_tensor("wf", [P, hp["wf_w"]], F32, kind="ExternalInput").ap()
    wi = nc.dram_tensor("wi", [P, hp["wi_w"]], I32, kind="ExternalInput").ap()
    sv = nc.dram_tensor("sv", [P, hp["sv_w"]], F32, kind="ExternalInput").ap()
    cst = nc.dram_tensor("cst", [P, hp["cst_w"]], F32, kind="ExternalInput").ap()
    y_t = nc.dram_tensor("y_t", [D, T], F32, kind="ExternalOutput").ap()
    xdel = nc.dram_tensor("xdel", [D, T], F32, kind="Internal").ap()

    with contextlib.ExitStack() as S:
        b = B(nc, S)
        b.dbg = dbg
        b.block = S.enter_context(nc.Block())
        LD, ST, PT, PV, PS = b.LD, b.ST, b.PT, b.PV, b.PS

        X = [b.sbuf(S, f"X{c}", [P, T], F16) for c in range(KCD)]
        m_b = b.sbuf(S, "m_b", [P, T], F16)
        r_b = b.sbuf(S, "r_b", [P, T], F16)
        cos2 = b.sbuf(S, "cos2", [P, T], F16)
        sin2 = b.sbuf(S, "sin2", [P, T], F16)
        bt = b.sbuf(S, "bt", [P, BTW], F16)
        ident = b.sbuf(S, "ident", [P, P], F16)
        pswap = b.sbuf(S, "pswap", [P, P], F16)
        ones_c = b.sbuf(S, "ones_c", [P, 1], F16)
        ones_r = b.sbuf(S, "ones_r", [1, P], F16)
        ones_r32 = b.sbuf(S, "ones_r32", [1, P], F32)
        sv_sb = b.sbuf(S, "sv_sb", [P, hp["sv_w"]], F32)
        rope_a = b.sbuf(S, "rope_a", [P, TTW], F16)
        rope_b = b.sbuf(S, "rope_b", [P, TTW], F16)
        M_run = b.sbuf(S, "M_run", [P, TTW], F16)
        mxcol = b.sbuf(S, "mxcol", [P, 4], F16)
        M_q = b.sbuf(S, "M_q", [1, TTW], F16)
        negmax = b.sbuf(S, "negmax", [P, TTW], F16)
        tmpsub = b.sbuf(S, "tmpsub", [P, 2 * TTW], F16)
        rcp_r = b.sbuf(S, "rcp_r", [1, TTW], F32)
        rcp_b = b.sbuf(S, "rcp_b", [P, TTW], F16)
        wo_sb = b.sbuf(S, "wo_sb", [P, 2 * TTW], F32)
        row16 = b.sbuf(S, "row16", [1, 2 * TTW], F16)
        r1sb = b.sbuf(S, "r1sb", [P, TTW], F16)
        rhsb = b.sbuf(S, "rhsb", [P, TTW], F16)
        gbuf = b.sbuf(S, "gbuf", [P, TTW], F16)
        gtmp = b.sbuf(S, "gtmp", [P, 2 * TTW], F16)
        kv = b.sbuf(S, "kvals", [P, 4], F32)

        x_marks = [b.dma(X[c][:], x_t[c * P:(c + 1) * P, :]) for c in range(KCD)]
        cm = {}
        for nm, dst in (("ident", ident), ("cos2", cos2), ("sin2", sin2),
                        ("pswap", pswap)):
            o, w = cst_reg[nm]
            cm[nm] = b.dma(dst[:], cst[:, o:o + w])
        sv_mark = b.dma(sv_sb[:], sv[:, 0:hp["sv_w"]])
        b.emit("vector", lambda e: e.memset(ones_c[:], 1.0))
        b.emit("vector", lambda e: e.memset(ones_r[:], 1.0))
        b.emit("vector", lambda e: e.memset(ones_r32[:], 1.0))
        b.emit("vector", lambda e: e.memset(kv[:, 0:1], EPS))
        b.emit("vector", lambda e: e.memset(kv[:, 1:2], -0.5))
        b.emit("vector", lambda e: e.memset(kv[:, 2:3], QSCALE), incs=[(PV, 1)])
        consts_pv = PV.n
        b.rope_pt = 0
        b.ln_ps = 0

        def emit_xn(xn, tt, free_mark):
            sl = slice(tt * TTW, (tt + 1) * TTW)
            for kc in range(KCD):
                w = [(PT, free_mark), (PS, b.ln_ps)] if kc == 0 else []
                b.emit("vector",
                       lambda e, kc=kc, sl=sl: e.tensor_sub(
                           xn[:, kc * TTW:(kc + 1) * TTW], X[kc][:, sl], m_b[:, sl]),
                       waits=w)
                b.emit("vector",
                       lambda e, kc=kc, sl=sl: e.tensor_mul(
                           xn[:, kc * TTW:(kc + 1) * TTW],
                           xn[:, kc * TTW:(kc + 1) * TTW], r_b[:, sl]),
                       incs=[(PV, 1)] if kc == KCD - 1 else ())
            return PV.n

        def emit_layernorm(lyr):
            pv_start = PV.n
            with contextlib.ExitStack() as S2:
                rows = b.sbuf(S2, "lnrows", [1, 4 * T], F32)
                sq = b.sbuf(S2, "sq", [P, 2 * TTW], F16)
                ps1 = b.psum(S2, "lnps1", [1, TTW], F32)
                ps2 = b.psum(S2, "lnps2", [1, TTW], F32)
                psb = b.psum(S2, "lnpsb", [P, TTW], F32)
                m_row = rows[:, 0:T]
                e_row = rows[:, T:2 * T]
                m2_row = rows[:, 2 * T:3 * T]
                r_row = rows[:, 3 * T:4 * T]
                sqm = []
                first_s = True
                first_t = True
                for tt in range(NTT):
                    sl = slice(tt * TTW, (tt + 1) * TTW)
                    evac_pv = PV.n
                    for kc in range(KCD):
                        n = tt * KCD + kc
                        sqs = sq[:, (kc % 2) * TTW:(kc % 2 + 1) * TTW]
                        w = []
                        if first_s:
                            w = [(LD, max(x_marks)), (PV, pv_start)]
                            first_s = False
                        if n >= 2:
                            w.append((PT, sqm[n - 2]))
                        b.emit("scalar",
                               lambda e, kc=kc, sl=sl, sqs=sqs: e.activation(
                                   sqs, X[kc][:, sl], AF.Square),
                               waits=w, incs=[(PS, 1)])
                        sq_ps = PS.n
                        w = []
                        if first_t:
                            w = [(PV, consts_pv), (PV, pv_start), (LD, max(x_marks))]
                            first_t = False
                        if kc == 0:
                            w.append((PV, evac_pv))
                        b.emit("tensor",
                               lambda e, kc=kc, sl=sl: e.matmul(
                                   ps1[:], ones_c[:], X[kc][:, sl],
                                   start=(kc == 0), stop=(kc == KCD - 1),
                                   skip_group_check=True),
                               waits=w)
                        b.emit("tensor",
                               lambda e, sqs=sqs, kc=kc: e.matmul(
                                   ps2[:], ones_c[:], sqs,
                                   start=(kc == 0), stop=(kc == KCD - 1),
                                   skip_group_check=True),
                               waits=[(PS, sq_ps)], incs=[(PT, 1)])
                        sqm.append(PT.n)
                    mmark = PT.n
                    b.emit("vector",
                           lambda e, sl=sl: e.tensor_scalar_mul(
                               m_row[0:1, sl], ps1[:], 1.0 / D),
                           waits=[(PT, mmark)])
                    b.emit("vector",
                           lambda e, sl=sl: e.tensor_scalar_mul(
                               e_row[0:1, sl], ps2[:], 1.0 / D),
                           incs=[(PV, 1)])
                b.emit("vector", lambda e: e.tensor_mul(m2_row, m_row, m_row))
                b.emit("vector", lambda e: e.tensor_sub(e_row, e_row, m2_row),
                       incs=[(PV, 1)])
                vmark = PV.n
                b.emit("scalar", lambda e: e.activation(m2_row, e_row, AF.Ln,
                                                         bias=kv[0:1, 0:1]),
                       waits=[(PV, vmark)])
                b.emit("scalar", lambda e: e.activation(r_row, m2_row, AF.Exp,
                                                        scale=kv[0:1, 1:2]),
                       incs=[(PS, 1)])
                for tt in range(NTT):
                    sl = slice(tt * TTW, (tt + 1) * TTW)
                    for src, dst in ((m_row, m_b), (r_row, r_b)):
                        b.emit("tensor",
                               lambda e, src=src, sl=sl: e.matmul(
                                   psb[:], ones_r32[:], src[0:1, sl],
                                   start=True, stop=True),
                               waits=[(PS, PS.n)], incs=[(PT, 1)])
                        b.emit("scalar",
                               lambda e, dst=dst, sl=sl: e.copy(dst[:, sl], psb[:]),
                               waits=[(PT, PT.n)], incs=[(PS, 1)])
                b.ln_ps = PS.n
                b.flush()

        def emit_rope(ap_tile, tt, ps_rope):
            sl = slice(tt * TTW, (tt + 1) * TTW)
            b.emit("vector",
                   lambda e: e.tensor_mul(rope_a[:], ap_tile, cos2[:, sl]),
                   waits=[(PT, b.rope_pt)])
            b.emit("vector",
                   lambda e: e.tensor_mul(rope_b[:], ap_tile, sin2[:, sl]),
                   incs=[(PV, 1)])
            ab_pv = PV.n
            b.emit("tensor",
                   lambda e: e.matmul(ps_rope[:], ident[:], rope_a[:],
                                      start=True, stop=False),
                   waits=[(PV, ab_pv)])
            b.emit("tensor",
                   lambda e: e.matmul(ps_rope[:], pswap[:], rope_b[:],
                                      start=False, stop=True),
                   incs=[(PT, 1)])
            b.rope_pt = PT.n
            b.emit("vector", lambda e: e.tensor_copy(ap_tile, ps_rope[:]),
                   waits=[(PT, b.rope_pt)], incs=[(PV, 1)])
            return PV.n

        def emit_attention(lyr):
            lk = "l0" if lyr == 0 else "l1"
            with contextlib.ExitStack() as SA:
                K4 = b.sbuf(SA, "K4", [P, GRP * T], F16)
                VT4 = b.sbuf(SA, "VT4", [P, GRP * T], F16)
                for g in range(NG):
                    emit_group(lyr, lk, g, K4, VT4)
                b.flush()

        def emit_group(lyr, lk, g, K4, VT4):
            if lyr == 1 or g == 0:
                with contextlib.ExitStack() as S2:
                    kvslab = b.sbuf(S2, "kvslab", [P, 8 * D], F16)
                    vtmp = b.sbuf(S2, "vtmp", [P, TTW], F16)
                    xn = b.sbuf(S2, "xn", [P, KCD * TTW], F16)
                    pkv = [b.psum(S2, f"pkv{i}", [P, TTW], F32) for i in range(2)]
                    ptr = [b.psum(S2, f"ptr{i}", [P, 1024], F16) for i in range(2)]
                    ps_rope = b.psum(S2, "ps_rope", [P, TTW], F32)
                    if lyr == 0:
                        koffs = [reg["l0_wk"][j] for j in range(4)]
                        voffs = [reg["l0_wv"][j] for j in range(4)]
                    else:
                        koffs = [reg["l1_wk"][GRP * g + j] for j in range(GRP)]
                        voffs = [reg["l1_wv"][GRP * g + j] for j in range(GRP)]
                    kv_marks = []
                    wprev = [(PT, PT.n)]
                    for j in range(4):
                        kv_marks.append(b.dma(kvslab[:, (2 * j) * D:(2 * j + 1) * D],
                                              wf[:, koffs[j]:koffs[j] + D], waits=wprev))
                        wprev = ()
                        kv_marks.append(b.dma(kvslab[:, (2 * j + 1) * D:(2 * j + 2) * D],
                                              wf[:, voffs[j]:voffs[j] + D]))
                    chain_mark = PT.n
                    for tt in range(NTT):
                        xn_ready = emit_xn(xn, tt, chain_mark)
                        for j in range(4):
                            for which in (0, 1):
                                bank = pkv[which]
                                slab = kvslab[:, (2 * j + which) * D:(2 * j + which + 1) * D]
                                evac_guard = PS.n
                                for kc in range(KCD):
                                    w = []
                                    if kc == 0:
                                        w = [(PV, xn_ready),
                                             (LD, kv_marks[2 * j + which]),
                                             (PS, evac_guard)]
                                    b.emit("tensor",
                                           lambda e, bank=bank, slab=slab, kc=kc: e.matmul(
                                               bank[:], slab[:, kc * P:(kc + 1) * P],
                                               xn[:, kc * TTW:(kc + 1) * TTW],
                                               start=(kc == 0), stop=(kc == KCD - 1)),
                                           waits=w,
                                           incs=[(PT, 1)] if kc == KCD - 1 else ())
                                chain_mark = PT.n
                                if which == 0:
                                    dst = K4[:, j * T + tt * TTW: j * T + (tt + 1) * TTW]
                                    b.emit("scalar",
                                           lambda e, dst=dst, bank=bank: e.copy(dst, bank[:]),
                                           waits=[(PT, chain_mark)], incs=[(PS, 1)])
                                    if lyr == 0:
                                        b.emit("vector", lambda e: e.engine_nop(),
                                               waits=[(PS, PS.n)])
                                        emit_rope(dst, tt, ps_rope)
                                else:
                                    b.emit("scalar",
                                           lambda e, bank=bank: e.copy(vtmp[:], bank[:]),
                                           waits=[(PT, PT.n)], incs=[(PS, 1)])
                                    vm = PS.n
                                    for j2 in range(4):
                                        w = [(PV, PV.n)]
                                        if j2 == 0:
                                            w.append((PS, vm))
                                        b.emit("tensor",
                                               lambda e, j2=j2: e.transpose(
                                                   ptr[j2 % 2][:, 0:P],
                                                   vtmp[:, j2 * P:(j2 + 1) * P], ident[:]),
                                               waits=w, incs=[(PT, 1)])
                                        dstv = VT4[:, j * T + (tt * 4 + j2) * P:
                                                   j * T + (tt * 4 + j2 + 1) * P]
                                        b.emit("vector",
                                               lambda e, dstv=dstv, j2=j2: e.tensor_copy(
                                                   dstv, ptr[j2 % 2][:, 0:P]),
                                               waits=[(PT, PT.n)], incs=[(PV, 1)])
                    b.flush()
                    if g == 0:
                        b.tap(f"K4_{lyr}", K4[:])
                        b.tap(f"VT4_{lyr}", VT4[:])

            with contextlib.ExitStack() as S2:
                Q4 = b.sbuf(S2, "Q4", [P, GRP * T], F16)
                O4 = b.sbuf(S2, "O4", [P, GRP * T], F16)
                with contextlib.ExitStack() as S3:
                    qslab = b.sbuf(S3, "qslab", [P, GRP * D], F16)
                    xn = b.sbuf(S3, "xnq", [P, KCD * TTW], F16)
                    pq = [b.psum(S3, f"pq{i}", [P, TTW], F32) for i in range(2)]
                    ps_rope = b.psum(S3, "ps_ropeq", [P, TTW], F32)
                    qoffs = [reg[f"{lk}_wq"][GRP * g + j] for j in range(GRP)]
                    q_marks = []
                    w0 = [(PT, PT.n)]
                    for j in range(GRP):
                        q_marks.append(b.dma(qslab[:, j * D:(j + 1) * D],
                                             wf[:, qoffs[j]:qoffs[j] + D], waits=w0))
                        w0 = ()
                    chain_mark = PT.n
                    for tt in range(NTT):
                        xn_ready = emit_xn(xn, tt, chain_mark)
                        for j in range(GRP):
                            bank = pq[j % 2]
                            evac_guard = PS.n - 1
                            for kc in range(KCD):
                                w = []
                                if kc == 0:
                                    w = [(PV, xn_ready), (LD, q_marks[j]),
                                         (PS, evac_guard)]
                                b.emit("tensor",
                                       lambda e, bank=bank, j=j, kc=kc: e.matmul(
                                           bank[:],
                                           qslab[:, j * D + kc * P:j * D + (kc + 1) * P],
                                           xn[:, kc * TTW:(kc + 1) * TTW],
                                           start=(kc == 0), stop=(kc == KCD - 1)),
                                       waits=w, incs=[(PT, 1)] if kc == KCD - 1 else ())
                            chain_mark = PT.n
                            dst = Q4[:, j * T + tt * TTW: j * T + (tt + 1) * TTW]
                            b.emit("scalar",
                                   lambda e, dst=dst, bank=bank: e.activation(
                                       dst, bank[:], AF.Copy, scale=kv[:, 2:3]),
                                   waits=[(PT, chain_mark)], incs=[(PS, 1)])
                            if lyr == 0:
                                b.emit("vector", lambda e: e.engine_nop(),
                                       waits=[(PS, PS.n)])
                                emit_rope(dst, tt, ps_rope)
                    b.flush()
                if g == 0:
                    b.tap(f"Q4_{lyr}", Q4[:])

                b.emit("tensor", lambda e: e.nop(),
                       waits=[(PS, PS.n), (PV, PV.n)], incs=[(PT, 1)])

                with contextlib.ExitStack() as S3:
                    Srow = b.sbuf(S3, "Srow", [P, 16 * TTW], F16)
                    ps_s = [b.psum(S3, f"ps_s{i}", [P, TTW], F32) for i in range(2)]
                    ps_tr = [b.psum(S3, f"ps_tr{i}", [P, 1024], F16) for i in range(2)]
                    ps_o = b.psum(S3, "ps_o", [P, TTW], F32)
                    ps_sum = b.psum(S3, "ps_sum", [1, TTW], F32)
                    ps_nm = b.psum(S3, "ps_nm", [P, TTW], F32)
                    ps_rb = b.psum(S3, "ps_rb", [P, TTW], F32)
                    for j in range(GRP):
                        h = GRP * g + j
                        kidx = (h // 4) if lyr == 0 else j
                        slope = float(2.0 ** (-(h + 1) / 2.0)) if lyr == 0 else 1.0
                        for qt in range(NTT):
                            emit_attn_hqt(j, kidx, slope, qt, K4, VT4, Q4, Srow, O4,
                                          ps_s, ps_tr, ps_o, ps_sum, ps_nm, ps_rb)
                    b.flush()
                    if g == 0:
                        b.tap(f"O4_{lyr}", O4[:])
                        b.tap(f"Srow_{lyr}", Srow[:])

                with contextlib.ExitStack() as S3:
                    offs = reg[f"{lk}_wo"][16 * g:16 * (g + 1)]
                    ws = WStream(b, S3, "wostr", wf, offs, GRP * P)
                    ps_w = [b.psum(S3, f"ps_w{i}", [P, TTW], F32) for i in range(2)]
                    ws.start()
                    for oc in range(KCD):
                        slab, ldm = ws.use(oc)
                        for tt in range(NTT):
                            bank = ps_w[(oc * NTT + tt) % 2]
                            evac_guard = PS.n - 1
                            for j in range(GRP):
                                w = []
                                if j == 0:
                                    w = [(LD, ldm), (PS, evac_guard)]
                                b.emit("tensor",
                                       lambda e, bank=bank, slab=slab, j=j, tt=tt: e.matmul(
                                           bank[:], slab[:, j * P:(j + 1) * P],
                                           O4[:, j * T + tt * TTW: j * T + (tt + 1) * TTW],
                                           start=(j == 0), stop=(j == GRP - 1)),
                                       waits=w, incs=[(PT, 1)] if j == GRP - 1 else ())
                            wbuf = wo_sb[:, ((oc * NTT + tt) % 2) * TTW:
                                         ((oc * NTT + tt) % 2 + 1) * TTW]
                            b.emit("scalar",
                                   lambda e, wbuf=wbuf, bank=bank: e.copy(wbuf, bank[:]),
                                   waits=[(PT, PT.n), (ST, max(0, ST.n - 16))],
                                   incs=[(PS, 1)])
                            b.dma(xdel[oc * P:(oc + 1) * P, tt * TTW:(tt + 1) * TTW],
                                  wbuf, waits=[(PS, PS.n)], accum=(g > 0), store=True)
                        ws.done(oc)
                    b.flush()

        def emit_attn_hqt(j, kidx, slope, qt, K4, VT4, Q4, Srow, O4,
                          ps_s, ps_tr, ps_o, ps_sum, ps_nm, ps_rb):
            nkt = 4 * (qt + 1)
            prev_av = PT.n
            sttm = []
            for kt in range(nkt):
                bank = ps_s[kt % 2]
                w = [(PV, sttm[kt - 2])] if kt >= 2 else []
                b.emit("tensor",
                       lambda e, bank=bank, kt=kt: e.matmul(
                           bank[:], K4[:, kidx * T + kt * P: kidx * T + (kt + 1) * P],
                           Q4[:, j * T + qt * TTW: j * T + (qt + 1) * TTW],
                           start=True, stop=True),
                       waits=w, incs=[(PT, 1)])
                mmark = PT.n
                j0 = 384 + qt * TTW - kt * P
                ssl = Srow[:, kt * TTW:(kt + 1) * TTW]
                w = [(PT, mmark)]
                if kt == 0:
                    w.append((PT, prev_av))
                b.emit("vector",
                       lambda e, ssl=ssl, bank=bank, j0=j0, slope=slope:
                       e.scalar_tensor_tensor(
                           ssl, bt[:, j0:j0 + TTW], slope, bank[:], ALU.mult, ALU.add),
                       waits=w, incs=[(PV, 1)])
                sttm.append(PV.n)
                if kt == 0:
                    b.emit("vector", lambda e, ssl=ssl: e.tensor_copy(M_run[:], ssl))
                else:
                    b.emit("vector",
                           lambda e, ssl=ssl: e.tensor_max(M_run[:], M_run[:], ssl),
                           incs=[(PV, 1)] if kt == nkt - 1 else ())
            runmax_pv = PV.n
            for j2 in range(4):
                w = [(PV, PV.n)] if j2 > 0 else [(PV, runmax_pv), (PT, PT.n)]
                b.emit("tensor",
                       lambda e, j2=j2: e.transpose(
                           ps_tr[j2 % 2][:, 0:P], M_run[:, j2 * P:(j2 + 1) * P], ident[:]),
                       waits=w, incs=[(PT, 1)])
                b.emit("vector",
                       lambda e, j2=j2: e.reduce_max(
                           mxcol[:, j2:j2 + 1], ps_tr[j2 % 2][0:P, 0:P], axis=AX.X),
                       waits=[(PT, PT.n)], incs=[(PV, 1)])
            red_pv = PV.n
            for j2 in range(4):
                w = [(PV, PV.n)] if j2 > 0 else [(PV, red_pv)]
                b.emit("tensor",
                       lambda e, j2=j2: e.transpose(
                           ps_tr[j2 % 2][0:1, 0:P], mxcol[:, j2:j2 + 1], ident[:]),
                       waits=w, incs=[(PT, 1)])
                b.emit("vector",
                       lambda e, j2=j2: e.tensor_scalar_mul(
                           M_q[0:1, j2 * P:(j2 + 1) * P], ps_tr[j2 % 2][0:1, 0:P], -1.0),
                       waits=[(PT, PT.n)], incs=[(PV, 1)])
            b.emit("tensor",
                   lambda e: e.matmul(ps_nm[:], ones_r[:], M_q[0:1, :],
                                      start=True, stop=True),
                   waits=[(PV, PV.n)], incs=[(PT, 1)])
            b.emit("scalar", lambda e: e.copy(negmax[:], ps_nm[:]),
                   waits=[(PT, PT.n)], incs=[(PS, 1)])
            nm_ps = PS.n
            for kt in range(nkt):
                ssl = Srow[:, kt * TTW:(kt + 1) * TTW]
                tsl = tmpsub[:, (kt % 2) * TTW:(kt % 2 + 1) * TTW]
                w = [(PS, nm_ps)] if kt == 0 else []
                if kt >= 2:
                    w.append((PS, PS.n - 1))
                b.emit("vector",
                       lambda e, ssl=ssl, tsl=tsl: e.tensor_add(tsl, ssl, negmax[:]),
                       waits=w, incs=[(PV, 1)])
                b.emit("scalar",
                       lambda e, ssl=ssl, tsl=tsl: e.activation(ssl, tsl, AF.Exp),
                       waits=[(PV, PV.n)], incs=[(PS, 1)])
                expm = PS.n
                b.emit("tensor",
                       lambda e, ssl=ssl, kt=kt: e.matmul(
                           ps_sum[:], ones_c[:], ssl,
                           start=(kt == 0), stop=(kt == nkt - 1), skip_group_check=True),
                       waits=[(PS, expm)])
                b.emit("tensor",
                       lambda e, ssl=ssl, kt=kt: e.matmul(
                           ps_o[:], VT4[:, kidx * T + kt * P: kidx * T + (kt + 1) * P],
                           ssl, start=(kt == 0), stop=(kt == nkt - 1),
                           skip_group_check=True),
                       incs=[(PT, 1)] if kt == nkt - 1 else ())
            av_mark = PT.n
            b.emit("vector", lambda e: e.reciprocal(rcp_r[0:1, :], ps_sum[:]),
                   waits=[(PT, av_mark)], incs=[(PV, 1)])
            b.emit("tensor",
                   lambda e: e.matmul(ps_rb[:], ones_r32[:], rcp_r[0:1, :],
                                      start=True, stop=True),
                   waits=[(PV, PV.n)], incs=[(PT, 1)])
            b.emit("scalar", lambda e: e.copy(rcp_b[:], ps_rb[:]),
                   waits=[(PT, PT.n)], incs=[(PS, 1)])
            osl = O4[:, j * T + qt * TTW: j * T + (qt + 1) * TTW]
            b.emit("vector",
                   lambda e, osl=osl: e.tensor_mul(osl, ps_o[:], rcp_b[:]),
                   waits=[(PS, PS.n)], incs=[(PV, 1)])
            b.emit("tensor", lambda e: e.nop(), waits=[(PV, PV.n)], incs=[(PT, 1)])

        def emit_ffn(lyr):
            lk = "l0" if lyr == 0 else "l1"
            quant = (lyr == 1)
            src = wi if quant else wf
            sg_o, _ = sv_reg[f"{lk}_gs"]
            su_o, _ = sv_reg[f"{lk}_us"]
            sd_o, _ = sv_reg[f"{lk}_ds"]
            with contextlib.ExitStack() as S2:
                xn = b.sbuf(S2, "xnf", [P, KCD * TTW], F16)
                hbuf = b.sbuf(S2, "hbuf", [P, NIC * TTW], F16)
                xd = b.sbuf(S2, "xd", [P, KCD * TTW], F16)
                ps_g = [b.psum(S2, f"fg{i}", [P, TTW], F32) for i in range(2)]
                ps_u = [b.psum(S2, f"fu{i}", [P, TTW], F32) for i in range(2)]
                ps_d = [b.psum(S2, f"fd{i}", [P, TTW], F32) for i in range(2)]
                ps_r = b.psum(S2, "fr", [P, TTW], F32)
                xdel_done = ST.n
                chain_mark = PT.n
                xd_guard = 0
                for tt in range(NTT):
                    xn_ready = emit_xn(xn, tt, chain_mark)
                    xd_marks = []
                    for kc in range(KCD):
                        w = [(ST, xdel_done)]
                        if kc == 0:
                            w += [(PV, xd_guard), (PT, chain_mark)]
                        xd_marks.append(b.dma(
                            xd[:, kc * TTW:(kc + 1) * TTW],
                            xdel[kc * P:(kc + 1) * P, tt * TTW:(tt + 1) * TTW],
                            waits=w))
                    if quant:
                        for kc in range(KCD):
                            w = [(PV, xn_ready), (PS, PS.n)] if kc == 0 else []
                            b.emit("tensor",
                                   lambda e, kc=kc: e.matmul(
                                       ps_r[0:1, :], ones_c[:],
                                       xn[:, kc * TTW:(kc + 1) * TTW],
                                       start=(kc == 0), stop=(kc == KCD - 1),
                                       skip_group_check=True),
                                   waits=w, incs=[(PT, 1)] if kc == KCD - 1 else ())
                        b.emit("scalar",
                               lambda e: e.copy(row16[0:1, 0:TTW], ps_r[0:1, :]),
                               waits=[(PT, PT.n)], incs=[(PS, 1)])
                        b.emit("tensor",
                               lambda e: e.matmul(ps_r[:], ones_r[:],
                                                  row16[0:1, 0:TTW],
                                                  start=True, stop=True),
                               waits=[(PS, PS.n)], incs=[(PT, 1)])
                        b.emit("scalar", lambda e: e.copy(r1sb[:], ps_r[:]),
                               waits=[(PT, PT.n)], incs=[(PS, 1)])
                    with contextlib.ExitStack() as S3:
                        gs = WStream(b, S3, "gstr", src, reg[f"{lk}_gate"], D)
                        us = WStream(b, S3, "ustr", src, reg[f"{lk}_up"], D)
                        gs.start()
                        us.start()
                        for ic in range(NIC):
                            for which in (0, 1):
                                stream = gs if which == 0 else us
                                bank = (ps_g if which == 0 else ps_u)[ic % 2]
                                slab, ldm = stream.use(ic)
                                if not quant and which == 0:
                                    guard = (PS, PS.n - 1)
                                else:
                                    guard = (PV, PV.n - 1)
                                for kc in range(KCD):
                                    w = []
                                    if kc == 0:
                                        w = [(LD, ldm), guard, (PV, xn_ready)]
                                    b.emit("tensor",
                                           lambda e, bank=bank, slab=slab, kc=kc: e.matmul(
                                               bank[:], slab[:, kc * P:(kc + 1) * P],
                                               xn[:, kc * TTW:(kc + 1) * TTW],
                                               start=(kc == 0), stop=(kc == KCD - 1)),
                                           waits=w,
                                           incs=[(PT, 1)] if kc == KCD - 1 else ())
                                cmk = PT.n
                                stream.done(ic)
                                sgap = sv_sb[:, sg_o + ic:sg_o + ic + 1]
                                suap = sv_sb[:, su_o + ic:su_o + ic + 1]
                                hsl = hbuf[:, ic * TTW:(ic + 1) * TTW]
                                if which == 0:
                                    if quant:
                                        gt = gtmp[:, 0:TTW]
                                        b.emit("vector",
                                               lambda e, gt=gt, bank=bank:
                                               e.scalar_tensor_tensor(
                                                   gt, r1sb[:], -128.0, bank[:],
                                                   ALU.mult, ALU.add),
                                               waits=[(PT, cmk), (PS, PS.n)],
                                               incs=[(PV, 1)])
                                        b.emit("scalar",
                                               lambda e, gt=gt, sgap=sgap: e.activation(
                                                   gbuf[:], gt, AF.Silu, scale=sgap),
                                               waits=[(PV, PV.n)], incs=[(PS, 1)])
                                    else:
                                        b.emit("scalar",
                                               lambda e, bank=bank, sgap=sgap: e.activation(
                                                   gbuf[:], bank[:], AF.Silu, scale=sgap),
                                               waits=[(PT, cmk), (LD, sv_mark),
                                                      (PV, PV.n)],
                                               incs=[(PS, 1)])
                                else:
                                    if quant:
                                        ut = gtmp[:, TTW:2 * TTW]
                                        b.emit("vector",
                                               lambda e, ut=ut, bank=bank:
                                               e.scalar_tensor_tensor(
                                                   ut, r1sb[:], -128.0, bank[:],
                                                   ALU.mult, ALU.add),
                                               waits=[(PT, cmk)])
                                        b.emit("vector",
                                               lambda e, ut=ut, hsl=hsl, suap=suap:
                                               e.scalar_tensor_tensor(
                                                   hsl, ut, suap, gbuf[:],
                                                   ALU.mult, ALU.mult),
                                               waits=[(PS, PS.n)], incs=[(PV, 1)])
                                    else:
                                        b.emit("vector",
                                               lambda e, bank=bank, hsl=hsl, suap=suap:
                                               e.scalar_tensor_tensor(
                                                   hsl, bank[:], suap, gbuf[:],
                                                   ALU.mult, ALU.mult),
                                               waits=[(PT, cmk), (PS, PS.n)],
                                               incs=[(PV, 1)])
                        b.flush()
                    hdone_pv = PV.n
                    if tt == 0:
                        b.tap(f"xd_{lyr}", xd[:])
                        b.tap(f"hb_{lyr}", hbuf[:])
                    if quant:
                        for ic in range(NIC):
                            w = [(PV, hdone_pv), (PS, PS.n)] if ic == 0 else []
                            b.emit("tensor",
                                   lambda e, ic=ic: e.matmul(
                                       ps_r[0:1, :], ones_c[:],
                                       hbuf[:, ic * TTW:(ic + 1) * TTW],
                                       start=(ic == 0), stop=(ic == NIC - 1),
                                       skip_group_check=True),
                                   waits=w, incs=[(PT, 1)] if ic == NIC - 1 else ())
                        b.emit("scalar",
                               lambda e: e.copy(row16[0:1, TTW:2 * TTW], ps_r[0:1, :]),
                               waits=[(PT, PT.n)], incs=[(PS, 1)])
                        b.emit("tensor",
                               lambda e: e.matmul(ps_r[:], ones_r[:],
                                                  row16[0:1, TTW:2 * TTW],
                                                  start=True, stop=True),
                               waits=[(PS, PS.n)], incs=[(PT, 1)])
                        b.emit("scalar", lambda e: e.copy(rhsb[:], ps_r[:]),
                               waits=[(PT, PT.n)], incs=[(PS, 1)])
                        rh_ps = PS.n
                    with contextlib.ExitStack() as S3:
                        ds = WStream(b, S3, "dstr", src, reg[f"{lk}_down"], INTER)
                        ds.start()
                        for oc in range(KCD):
                            slab, ldm = ds.use(oc)
                            bank = ps_d[oc % 2]
                            guard = PV.n - 1
                            for ic in range(NIC):
                                w = []
                                if ic == 0:
                                    w = [(LD, ldm), (PV, hdone_pv), (PV, guard)]
                                b.emit("tensor",
                                       lambda e, bank=bank, slab=slab, ic=ic: e.matmul(
                                           bank[:], slab[:, ic * P:(ic + 1) * P],
                                           hbuf[:, ic * TTW:(ic + 1) * TTW],
                                           start=(ic == 0), stop=(ic == NIC - 1)),
                                       waits=w, incs=[(PT, 1)] if ic == NIC - 1 else ())
                            cmk = PT.n
                            ds.done(oc)
                            sdap = sv_sb[:, sd_o + oc:sd_o + oc + 1]
                            xsl = X[oc][:, tt * TTW:(tt + 1) * TTW]
                            xdsl = xd[:, oc * TTW:(oc + 1) * TTW]
                            gt = gtmp[:, 0:TTW]
                            if quant:
                                b.emit("vector",
                                       lambda e, gt=gt, bank=bank: e.scalar_tensor_tensor(
                                           gt, rhsb[:], -128.0, bank[:],
                                           ALU.mult, ALU.add),
                                       waits=[(PT, cmk), (PS, rh_ps)])
                                b.emit("vector",
                                       lambda e, gt=gt, sdap=sdap, xdsl=xdsl:
                                       e.scalar_tensor_tensor(
                                           gt, gt, sdap, xdsl, ALU.mult, ALU.add),
                                       waits=[(LD, xd_marks[oc])])
                                b.emit("vector",
                                       lambda e, gt=gt, xsl=xsl: e.tensor_add(
                                           xsl, gt, xsl),
                                       incs=[(PV, 1)])
                            else:
                                b.emit("vector",
                                       lambda e, bank=bank, gt=gt, sdap=sdap, xdsl=xdsl:
                                       e.scalar_tensor_tensor(
                                           gt, bank[:], sdap, xdsl, ALU.mult, ALU.add),
                                       waits=[(PT, cmk), (LD, xd_marks[oc])])
                                b.emit("vector",
                                       lambda e, gt=gt, xsl=xsl: e.tensor_add(
                                           xsl, gt, xsl),
                                       incs=[(PV, 1)])
                        b.flush()
                    chain_mark = PT.n
                    xd_guard = PV.n
                b.flush()

        for lyr in layers:
            o, w = cst_reg["bt0" if lyr == 0 else "bt1"]
            b.dma(bt[:], cst[:, o:o + w], waits=[(PV, PV.n)])
            emit_layernorm(lyr)
            b.tap(f"mb{lyr}", m_b[:])
            b.tap(f"rb{lyr}", r_b[:])
            emit_attention(lyr)
            emit_ffn(lyr)
            for c in range(KCD):
                b.tap(f"X{lyr}_{c}", X[c][:])

        final_pv = PV.n
        for c in range(KCD):
            b.dma(y_t[c * P:(c + 1) * P, :], X[c][:], waits=[(PV, final_pv)], store=True)
        b.emit("gpsimd", lambda e: e.nop(), waits=[(ST, ST.n)])
        b.flush()
    return nc



def kernel(**inputs):
    hp0 = prepare(inputs)
    hp = dict(reg=hp0["reg"], sv_reg=hp0["sv_reg"], cst_reg=hp0["cst_reg"],
              wf_w=hp0["wf"].shape[1], wi_w=hp0["wi"].shape[1],
              sv_w=hp0["sv"].shape[1], cst_w=hp0["cst"].shape[1])
    nc = build_program(hp, layers=(0, 1))
    x = np.asarray(inputs["L0_attn_x"])
    in_maps = []
    for core in range(8):
        xt = np.ascontiguousarray(x[core].T)
        in_maps.append(dict(x_t=xt, wf=hp0["wf"], wi=hp0["wi"],
                            sv=hp0["sv"], cst=hp0["cst"]))
    res = run_bass_kernel_spmd(nc, in_maps, core_ids=list(range(8)))
    out = np.empty((8, T, D), dtype=np.float32)
    for core in range(8):
        out[core] = np.asarray(res.results[core]["y_t"]).T
    return out


# version 02
# speedup vs baseline: 1.0947x; 1.0947x over previous
import contextlib
import numpy as np
import concourse.bass as bass
import concourse.mybir as mybir
from concourse.bass_utils import run_bass_kernel_spmd

F32 = mybir.dt.float32
F16 = mybir.dt.float16
I32 = mybir.dt.int32
AF = mybir.ActivationFunctionType
ALU = mybir.AluOpType
AX = mybir.AxisListType

P = 128
T = 2048
D = 2048
KCD = D // P
NTT = 4
TTW = 512
INTER = 5632
NIC = INTER // P
NH = 16
HD = 128
GRP = 4
NG = NH // GRP
QSCALE = float(1.0 / np.sqrt(128.0))
BIGNEG = -60000.0
BTW = 2432
EPS = 1e-5
ROPE_BASE = 10000.0
SROW_W = 16 * TTW
BIASMM = False



def _blockT(WT):
    K, M = WT.shape
    kc, mc = K // P, M // P
    return np.ascontiguousarray(
        WT.reshape(kc, P, mc, P).transpose(2, 1, 0, 3).reshape(mc, P, kc * P))


def _block_grouped(WT):
    Fdim, M = WT.shape
    mc = M // P
    return np.ascontiguousarray(
        WT.reshape(NG, GRP, P, mc, P).transpose(0, 3, 2, 1, 4).reshape(NG, mc, P, GRP * P))


class _Flat:
    def __init__(self, np_dtype):
        self.parts = []
        self.off = 0
        self.dtype = np_dtype

    def add(self, arr):
        a = arr.reshape(-1, P, arr.shape[-1])
        offs = []
        for s in a:
            self.parts.append(np.ascontiguousarray(s, dtype=self.dtype))
            offs.append(self.off)
            self.off += s.shape[-1]
        return offs

    def finalize(self):
        if not self.parts:
            return np.zeros((P, 1), dtype=self.dtype)
        return np.ascontiguousarray(np.concatenate(self.parts, axis=1))


def _host_consts():
    half = HD // 2
    theta = 1.0 / ROPE_BASE ** (np.arange(half, dtype=np.float64) / half)
    m = np.arange(T, dtype=np.float64)
    fr = np.outer(m, theta)
    cos = np.cos(fr).T.astype(np.float32)
    sin = np.sin(fr).T.astype(np.float32)
    cos2 = np.concatenate([cos, cos], axis=0)
    sin2 = np.concatenate([sin, -sin], axis=0)
    pswap = np.zeros((P, P), dtype=np.float32)
    pswap[np.arange(P), (np.arange(P) + 64) % P] = 1.0
    pp = np.arange(P)[:, None]
    jj = np.arange(BTW)[None, :]
    dist = (pp - jj + 384).astype(np.float32)
    bt0 = np.where(dist <= 0, dist, BIGNEG).astype(np.float32)
    bt1 = np.where(dist <= 0, 0.0, BIGNEG).astype(np.float32)
    ident = np.eye(P, dtype=np.float32)
    slopei = np.concatenate(
        [np.eye(P, dtype=np.float32) * (2.0 ** (-(h + 1) / 2.0)) for h in range(NH)],
        axis=1)
    cols = [ident, cos2, sin2, bt0, bt1, pswap, slopei]
    offs = np.cumsum([0] + [c.shape[1] for c in cols])
    return np.ascontiguousarray(np.concatenate(cols, axis=1)), {
        "ident": (int(offs[0]), P),
        "cos2": (int(offs[1]), T),
        "sin2": (int(offs[2]), T),
        "bt0": (int(offs[3]), BTW),
        "bt1": (int(offs[4]), BTW),
        "pswap": (int(offs[5]), P),
        "slopei": (int(offs[6]), NH * P),
    }


def prepare(inputs):
    wf = _Flat(np.float32)
    wi = _Flat(np.int32)
    reg = {}
    reg["l0_wq"] = wf.add(_blockT(np.asarray(inputs["L0_attn_Wq"])))
    reg["l0_wk"] = wf.add(_blockT(np.asarray(inputs["L0_attn_Wk"])))
    reg["l0_wv"] = wf.add(_blockT(np.asarray(inputs["L0_attn_Wv"])))
    reg["l0_wo"] = wf.add(_block_grouped(np.asarray(inputs["L0_attn_Wo"])))
    reg["l0_gate"] = wf.add(_blockT(np.asarray(inputs["L0_ffn_gate_w"]).T))
    reg["l0_up"] = wf.add(_blockT(np.asarray(inputs["L0_ffn_up_w"]).T))
    reg["l0_down"] = wf.add(_blockT(np.asarray(inputs["L0_ffn_down_w"]).T))
    for nm, key in (("l1_wq", "L1_attn_W_Q"), ("l1_wk", "L1_attn_W_K"),
                    ("l1_wv", "L1_attn_W_V")):
        W = np.asarray(inputs[key])
        WT = W.transpose(2, 0, 1).reshape(D, NH * HD)
        reg[nm] = wf.add(_blockT(WT))
    reg["l1_wo"] = wf.add(_block_grouped(np.asarray(inputs["L1_attn_W_O"]).T))
    reg["l1_gate"] = wi.add(_blockT(np.asarray(inputs["L1_ffn_wq_gate"]).T))
    reg["l1_up"] = wi.add(_blockT(np.asarray(inputs["L1_ffn_wq_up"]).T))
    reg["l1_down"] = wi.add(_blockT(np.asarray(inputs["L1_ffn_wq_down"]).T))
    sv_cols = []
    sv_reg = {}

    def add_scale(name, vec):
        v = np.asarray(vec, dtype=np.float32).reshape(-1, P).T
        sv_reg[name] = (sum(c.shape[1] for c in sv_cols), v.shape[1])
        sv_cols.append(np.ascontiguousarray(v))

    add_scale("l0_gs", inputs["L0_ffn_gate_s"])
    add_scale("l0_us", inputs["L0_ffn_up_s"])
    add_scale("l0_ds", inputs["L0_ffn_down_s"])
    add_scale("l1_gs", inputs["L1_ffn_s_gate"])
    add_scale("l1_us", inputs["L1_ffn_s_up"])
    add_scale("l1_ds", inputs["L1_ffn_s_down"])
    sv = np.ascontiguousarray(np.concatenate(sv_cols, axis=1))
    cst, cst_reg = _host_consts()
    return dict(wf=wf.finalize(), wi=wi.finalize(), sv=sv, cst=cst,
                reg=reg, sv_reg=sv_reg, cst_reg=cst_reg)



class Sem:
    __slots__ = ("h", "n")

    def __init__(self, h):
        self.h = h
        self.n = 0


class B:
    ENGS = ("tensor", "vector", "scalar", "gpsimd", "sync")

    def __init__(self, nc, stack):
        self.nc = nc
        self.block = None
        self.uid = 0
        self.q = {e: [] for e in self.ENGS}
        self.LD = Sem(stack.enter_context(nc.semaphore("LD")))
        self.ST = Sem(stack.enter_context(nc.semaphore("ST")))
        self.PT = Sem(stack.enter_context(nc.semaphore("PT")))
        self.PV = Sem(stack.enter_context(nc.semaphore("PV")))
        self.PS = Sem(stack.enter_context(nc.semaphore("PS")))
        self.PG = Sem(stack.enter_context(nc.semaphore("PG")))
        self._stack = stack
        self._dsems = {}
        self.dbg = False
        self.tapnames = []

    def dsem(self, name):
        if name not in self._dsems:
            self._dsems[name] = Sem(self._stack.enter_context(self.nc.semaphore(name)))
        return self._dsems[name]

    def sbuf(self, stack, name, shape, dtype):
        self.uid += 1
        return stack.enter_context(
            self.nc.sbuf_tensor(f"{name}_{self.uid}", shape, dtype))

    def psum(self, stack, name, shape, dtype):
        self.uid += 1
        return stack.enter_context(
            self.nc.psum_tensor(f"{name}_{self.uid}", shape, dtype))

    def emit(self, eng, fn, waits=(), incs=()):
        for s, v in waits:
            if v > 0:
                self.q[eng].append(lambda e, s=s, v=v: e.wait_ge(s.h, v))

        def _run(e, fn=fn, incs=tuple(incs)):
            ins = fn(e)
            for s, k in incs:
                ins = ins.then_inc(s.h, k)
        self.q[eng].append(_run)
        for s, k in incs:
            s.n += k

    def tap(self, name, ap):
        if not self.dbg:
            return
        shp = [int(x) for x in ap.shape]
        d = self.nc.dram_tensor(f"dbg_{name}", shp, F32, kind="ExternalOutput").ap()
        self.dma(d, ap, waits=[(self.PT, self.PT.n), (self.PV, self.PV.n),
                               (self.PS, self.PS.n)], sem=self.dsem("dbg"))
        self.tapnames.append(f"dbg_{name}")

    def dma(self, out, in_, waits=(), accum=False, sem=None):
        if sem is None:
            sem = self.LD

        def f(e, out=out, in_=in_, accum=accum):
            if accum:
                return e.dma_start(out=out, in_=in_, accum_op=ALU.add)
            return e.dma_start(out=out, in_=in_)
        self.emit("gpsimd", f, waits=waits, incs=[(sem, 16)])
        return (sem, sem.n)

    def flush(self):
        for name in self.ENGS:
            ops = self.q[name]
            if ops:
                self.q[name] = []
                getattr(self.block, name)(lambda e, ops=ops: [op(e) for op in ops])


class WStream:

    def __init__(self, b, stack, name, src, offs, width, dtype=F16):
        self.b = b
        self.buf = b.sbuf(stack, name, [P, 2 * width], dtype)
        self.sems = (b.dsem(f"{name}0"), b.dsem(f"{name}1"))
        self.src = src
        self.offs = offs
        self.width = width
        self.marks = {}
        self.n = len(offs)

    def start(self):
        w0 = [(self.b.PT, self.b.PT.n)]
        for j in range(min(2, self.n)):
            self.marks[j] = self.b.dma(
                self.buf[:, (j % 2) * self.width:(j % 2 + 1) * self.width],
                self.src[:, self.offs[j]:self.offs[j] + self.width], waits=w0,
                sem=self.sems[j % 2])
            w0 = ()

    def use(self, j):
        w = self.width
        return self.buf[:, (j % 2) * w:(j % 2 + 1) * w], self.marks[j]

    def done(self, j):
        if j + 2 < self.n:
            self.marks[j + 2] = self.b.dma(
                self.buf[:, ((j + 2) % 2) * self.width:((j + 2) % 2 + 1) * self.width],
                self.src[:, self.offs[j + 2]:self.offs[j + 2] + self.width],
                waits=[(self.b.PT, self.b.PT.n)], sem=self.sems[j % 2])


def build_program(hp, layers=(0, 1), dbg=False):
    nc = bass.Bass("TRN2", target_bir_lowering=False)
    reg, sv_reg, cst_reg = hp["reg"], hp["sv_reg"], hp["cst_reg"]

    x_t = nc.dram_tensor("x_t", [D, T], F32, kind="ExternalInput").ap()
    wf = nc.dram_tensor("wf", [P, hp["wf_w"]], F32, kind="ExternalInput").ap()
    wi = nc.dram_tensor("wi", [P, hp["wi_w"]], I32, kind="ExternalInput").ap()
    sv = nc.dram_tensor("sv", [P, hp["sv_w"]], F32, kind="ExternalInput").ap()
    cst = nc.dram_tensor("cst", [P, hp["cst_w"]], F32, kind="ExternalInput").ap()
    y_t = nc.dram_tensor("y_t", [D, T], F32, kind="ExternalOutput").ap()
    xdel = nc.dram_tensor("xdel", [D, T], F32, kind="Internal").ap()

    with contextlib.ExitStack() as S:
        b = B(nc, S)
        b.dbg = dbg
        b.block = S.enter_context(nc.Block())
        LD, ST, PT, PV, PS, PG = b.LD, b.ST, b.PT, b.PV, b.PS, b.PG

        X = [b.sbuf(S, f"X{c}", [P, T], F16) for c in range(KCD)]
        m_b = b.sbuf(S, "m_b", [P, T], F16)
        r_b = b.sbuf(S, "r_b", [P, T], F16)
        cos2 = b.sbuf(S, "cos2", [P, T], F16)
        sin2 = b.sbuf(S, "sin2", [P, T], F16)
        bt = b.sbuf(S, "bt", [P, BTW], F16)
        ident = b.sbuf(S, "ident", [P, P], F16)
        pswap = b.sbuf(S, "pswap", [P, P], F16)
        slopei = b.sbuf(S, "slopei", [P, NH * P], F16) if BIASMM else None
        ones_c = b.sbuf(S, "ones_c", [P, 1], F16)
        ones_r = b.sbuf(S, "ones_r", [1, P], F16)
        ones_r32 = b.sbuf(S, "ones_r32", [1, P], F32)
        sv_sb = b.sbuf(S, "sv_sb", [P, hp["sv_w"]], F32)
        rope_a = b.sbuf(S, "rope_a", [P, TTW], F16)
        rope_b = b.sbuf(S, "rope_b", [P, TTW], F16)
        M_run = b.sbuf(S, "M_run", [P, 2 * TTW], F16)
        mxcol = b.sbuf(S, "mxcol", [P, 8], F16)
        M_q = b.sbuf(S, "M_q", [1, 2 * TTW], F16)
        negmax = b.sbuf(S, "negmax", [P, 2 * TTW], F16)
        tmpsub = b.sbuf(S, "tmpsub", [P, 2 * TTW], F16)
        lnr = b.sbuf(S, "lnr", [1, TTW], F16)
        rcp_r = b.sbuf(S, "rcp_r", [1, TTW], F16)
        rcp_b = b.sbuf(S, "rcp_b", [P, TTW], F16)
        row16 = b.sbuf(S, "row16", [1, 2 * TTW], F16)
        r1sb = b.sbuf(S, "r1sb", [P, TTW], F16)
        rhsb = b.sbuf(S, "rhsb", [P, TTW], F16)
        gbuf = b.sbuf(S, "gbuf", [P, TTW], F16)
        gtmp = b.sbuf(S, "gtmp", [P, 2 * TTW], F16)
        kv = b.sbuf(S, "kvals", [P, 4], F32)

        XL = b.dsem("xl")
        CS = b.dsem("cs")
        for c in range(KCD):
            b.dma(X[c][:], x_t[c * P:(c + 1) * P, :], sem=XL)
        x_full = (XL, XL.n)
        for nm, dst in [("ident", ident), ("cos2", cos2), ("sin2", sin2),
                        ("pswap", pswap)] + ([("slopei", slopei)] if BIASMM else []):
            o, w = cst_reg[nm]
            b.dma(dst[:], cst[:, o:o + w], sem=CS)
        b.dma(sv_sb[:], sv[:, 0:hp["sv_w"]], sem=CS)
        b.cst_mark = (CS, CS.n)
        sv_mark = b.cst_mark
        b.emit("vector", lambda e: e.memset(ones_c[:], 1.0))
        b.emit("vector", lambda e: e.memset(ones_r[:], 1.0))
        b.emit("vector", lambda e: e.memset(ones_r32[:], 1.0))
        b.emit("vector", lambda e: e.memset(kv[:, 0:1], EPS))
        b.emit("vector", lambda e: e.memset(kv[:, 1:2], -0.5))
        b.emit("vector", lambda e: e.memset(kv[:, 2:3], QSCALE))
        b.emit("vector", lambda e: e.memset(kv[:, 3:4], -1.0), incs=[(PV, 1)])
        consts_pv = PV.n
        b.rope_pt = 0
        b.ln_ps = 0
        b.wo_guard = []

        def fence():
            b.emit("tensor", lambda e: e.nop(),
                   waits=[(PV, PV.n), (PS, PS.n), (ST, ST.n), (LD, LD.n)])
            b.emit("vector", lambda e: e.engine_nop(),
                   waits=[(PT, PT.n), (PS, PS.n), (ST, ST.n)])
            b.emit("scalar", lambda e: e.nop(),
                   waits=[(PT, PT.n), (PV, PV.n), (ST, ST.n)])
            b.emit("gpsimd", lambda e: e.nop(),
                   waits=[(PT, PT.n), (PV, PV.n), (PS, PS.n)])

        def emit_xn(xn, tt, free_mark):
            sl = slice(tt * TTW, (tt + 1) * TTW)
            marks = []
            for kc in range(KCD):
                w = (([(PT, free_mark), (PS, b.ln_ps)] + list(b.wo_guard))
                     if kc == 0 else [])
                b.emit("vector",
                       lambda e, kc=kc, sl=sl: e.tensor_sub(
                           xn[:, kc * TTW:(kc + 1) * TTW], X[kc][:, sl], m_b[:, sl]),
                       waits=w)
                b.emit("vector",
                       lambda e, kc=kc, sl=sl: e.tensor_mul(
                           xn[:, kc * TTW:(kc + 1) * TTW],
                           xn[:, kc * TTW:(kc + 1) * TTW], r_b[:, sl]),
                       incs=[(PV, 1)])
                marks.append(PV.n)
            return PV.n, marks

        def emit_layernorm(lyr):
            pv_start = PV.n
            with contextlib.ExitStack() as S2:
                rows = b.sbuf(S2, "lnrows", [1, 4 * T], F32)
                sq = b.sbuf(S2, "sq", [P, 2 * TTW], F16)
                ps1 = b.psum(S2, "lnps1", [1, TTW], F32)
                ps2 = b.psum(S2, "lnps2", [1, TTW], F32)
                psb = b.psum(S2, "lnpsb", [P, TTW], F32)
                m_row = rows[:, 0:T]
                e_row = rows[:, T:2 * T]
                m2_row = rows[:, 2 * T:3 * T]
                r_row = rows[:, 3 * T:4 * T]
                sqm = []
                first_s = True
                first_t = True
                for tt in range(NTT):
                    sl = slice(tt * TTW, (tt + 1) * TTW)
                    evac_pv = PV.n
                    for kc in range(KCD):
                        n = tt * KCD + kc
                        sqs = sq[:, (kc % 2) * TTW:(kc % 2 + 1) * TTW]
                        w = []
                        if first_s:
                            w = [x_full, (PV, pv_start)]
                            first_s = False
                        if n >= 2:
                            w.append((PT, sqm[n - 2]))
                        b.emit("scalar",
                               lambda e, kc=kc, sl=sl, sqs=sqs: e.activation(
                                   sqs, X[kc][:, sl], AF.Square),
                               waits=w, incs=[(PS, 1)])
                        sq_ps = PS.n
                        w = []
                        if first_t:
                            w = [(PV, consts_pv), (PV, pv_start), x_full]
                            first_t = False
                        if kc == 0:
                            w.append((PV, evac_pv))
                        b.emit("tensor",
                               lambda e, kc=kc, sl=sl: e.matmul(
                                   ps1[:], ones_c[:], X[kc][:, sl],
                                   start=(kc == 0), stop=(kc == KCD - 1),
                                   skip_group_check=True),
                               waits=w)
                        b.emit("tensor",
                               lambda e, sqs=sqs, kc=kc: e.matmul(
                                   ps2[:], ones_c[:], sqs,
                                   start=(kc == 0), stop=(kc == KCD - 1),
                                   skip_group_check=True),
                               waits=[(PS, sq_ps)], incs=[(PT, 1)])
                        sqm.append(PT.n)
                    mmark = PT.n
                    b.emit("vector",
                           lambda e, sl=sl: e.tensor_scalar_mul(
                               m_row[0:1, sl], ps1[:], 1.0 / D),
                           waits=[(PT, mmark)])
                    b.emit("vector",
                           lambda e, sl=sl: e.tensor_scalar_mul(
                               e_row[0:1, sl], ps2[:], 1.0 / D),
                           incs=[(PV, 1)])
                b.emit("vector", lambda e: e.tensor_mul(m2_row, m_row, m_row))
                b.emit("vector", lambda e: e.tensor_sub(e_row, e_row, m2_row),
                       incs=[(PV, 1)])
                vmark = PV.n
                b.emit("scalar", lambda e: e.activation(m2_row, e_row, AF.Ln,
                                                        bias=kv[0:1, 0:1]),
                       waits=[(PV, vmark)])
                b.emit("scalar", lambda e: e.activation(r_row, m2_row, AF.Exp,
                                                        scale=kv[0:1, 1:2]),
                       incs=[(PS, 1)])
                for tt in range(NTT):
                    sl = slice(tt * TTW, (tt + 1) * TTW)
                    for src, dst in ((m_row, m_b), (r_row, r_b)):
                        b.emit("tensor",
                               lambda e, src=src, sl=sl: e.matmul(
                                   psb[:], ones_r32[:], src[0:1, sl],
                                   start=True, stop=True),
                               waits=[(PS, PS.n)], incs=[(PT, 1)])
                        b.emit("scalar",
                               lambda e, dst=dst, sl=sl: e.copy(dst[:, sl], psb[:]),
                               waits=[(PT, PT.n)], incs=[(PS, 1)])
                b.ln_ps = PS.n
                b.flush()

        def emit_rope(ap_tile, tt, ps_rope, evac_ps):
            sl = slice(tt * TTW, (tt + 1) * TTW)
            b.emit("vector",
                   lambda e: e.tensor_mul(rope_a[:], ap_tile, cos2[:, sl]),
                   waits=[(PT, b.rope_pt), (PS, evac_ps), b.cst_mark])
            b.emit("vector",
                   lambda e: e.tensor_mul(rope_b[:], ap_tile, sin2[:, sl]),
                   incs=[(PV, 1)])
            ab_pv = PV.n
            b.emit("tensor",
                   lambda e: e.matmul(ps_rope[:], ident[:], rope_a[:],
                                      start=True, stop=False),
                   waits=[(PV, ab_pv)])
            b.emit("tensor",
                   lambda e: e.matmul(ps_rope[:], pswap[:], rope_b[:],
                                      start=False, stop=True),
                   incs=[(PT, 1)])
            b.rope_pt = PT.n
            b.emit("vector", lambda e: e.tensor_copy(ap_tile, ps_rope[:]),
                   waits=[(PT, b.rope_pt)], incs=[(PV, 1)])
            return PV.n

        def emit_attention(lyr):
            lk = "l0" if lyr == 0 else "l1"
            with contextlib.ExitStack() as SA:
                K4 = b.sbuf(SA, "K4", [P, GRP * T], F16)
                VT4 = b.sbuf(SA, "VT4", [P, GRP * T], F16)
                for g in range(NG):
                    emit_group(lyr, lk, g, K4, VT4)
                b.flush()

        def emit_group(lyr, lk, g, K4, VT4):
            if lyr == 1 or g == 0:
                with contextlib.ExitStack() as S2:
                    kvslab = b.sbuf(S2, "kvslab", [P, 8 * D], F16)
                    vtmp = b.sbuf(S2, "vtmp", [P, 2 * TTW], F16)
                    xn = b.sbuf(S2, "xn", [P, KCD * TTW], F16)
                    pkv = [b.psum(S2, f"pkv{i}", [P, TTW], F32) for i in range(2)]
                    ptr = [b.psum(S2, f"ptr{i}", [P, 1024], F16) for i in range(2)]
                    ps_rope = b.psum(S2, "ps_rope", [P, TTW], F32)
                    if lyr == 0:
                        koffs = [reg["l0_wk"][j] for j in range(4)]
                        voffs = [reg["l0_wv"][j] for j in range(4)]
                    else:
                        koffs = [reg["l1_wk"][GRP * g + j] for j in range(GRP)]
                        voffs = [reg["l1_wv"][GRP * g + j] for j in range(GRP)]
                    KV = b.dsem("kv")
                    wprev = [(PT, PT.n)]
                    for j in range(4):
                        b.dma(kvslab[:, (2 * j) * D:(2 * j + 1) * D],
                              wf[:, koffs[j]:koffs[j] + D], waits=wprev, sem=KV)
                        wprev = ()
                        b.dma(kvslab[:, (2 * j + 1) * D:(2 * j + 2) * D],
                              wf[:, voffs[j]:voffs[j] + D], sem=KV)
                    kv_full = (KV, KV.n)
                    chain_mark = PT.n
                    vj = 0
                    pend_tr = None
                    tr_marks = {}
                    for tt in range(NTT):
                        xn_ready, xn_marks = emit_xn(xn, tt, chain_mark)
                        pend_rope = []
                        first_chain = True
                        for j in range(4):
                            for which in (0, 1):
                                bank = pkv[which]
                                slab = kvslab[:, (2 * j + which) * D:(2 * j + which + 1) * D]
                                evac_guard = PS.n
                                for kc in range(KCD):
                                    w = []
                                    if first_chain:
                                        w.append((PV, xn_marks[kc]))
                                    if kc == 0:
                                        w += [kv_full, (PS, evac_guard), b.cst_mark]
                                        if not first_chain:
                                            w.append((PV, xn_ready))
                                    b.emit("tensor",
                                           lambda e, bank=bank, slab=slab, kc=kc: e.matmul(
                                               bank[:], slab[:, kc * P:(kc + 1) * P],
                                               xn[:, kc * TTW:(kc + 1) * TTW],
                                               start=(kc == 0), stop=(kc == KCD - 1)),
                                           waits=w,
                                           incs=[(PT, 1)] if kc == KCD - 1 else ())
                                first_chain = False
                                chain_mark = PT.n
                                if which == 0:
                                    dst = K4[:, j * T + tt * TTW: j * T + (tt + 1) * TTW]
                                    b.emit("scalar",
                                           lambda e, dst=dst, bank=bank: e.copy(dst, bank[:]),
                                           waits=[(PT, chain_mark)], incs=[(PS, 1)])
                                    if lyr == 0:
                                        pend_rope.append((dst, tt, PS.n))
                                else:
                                    vsl = vtmp[:, (vj % 2) * TTW:(vj % 2 + 1) * TTW]
                                    w = [(PT, chain_mark)]
                                    if vj >= 2:
                                        w.append((PT, tr_marks[vj - 2]))
                                    b.emit("scalar",
                                           lambda e, vsl=vsl, bank=bank: e.copy(vsl, bank[:]),
                                           waits=w, incs=[(PS, 1)])
                                    vm = PS.n
                                    if pend_tr is not None:
                                        emit_vtr(*pend_tr, VT4, ptr, tr_marks)
                                    pend_tr = (vj, j, tt, vm, vtmp)
                                    vj += 1
                        if pend_tr is not None:
                            emit_vtr(*pend_tr, VT4, ptr, tr_marks)
                            pend_tr = None
                        for dst, rtt, eps_ in pend_rope:
                            emit_rope(dst, rtt, ps_rope, eps_)
                    b.flush()
                    if dbg and g == 0:
                        b.tap(f"K4_{lyr}", K4[:])
                        b.tap(f"VT4_{lyr}", VT4[:])

            with contextlib.ExitStack() as S2:
                Q4 = b.sbuf(S2, "Q4", [P, GRP * T], F16)
                O4 = b.sbuf(S2, "O4", [P, GRP * T], F16)
                with contextlib.ExitStack() as S3:
                    qslab = b.sbuf(S3, "qslab", [P, GRP * D], F16)
                    xn = b.sbuf(S3, "xnq", [P, KCD * TTW], F16)
                    pq = [b.psum(S3, f"pq{i}", [P, TTW], F32) for i in range(2)]
                    ps_rope = b.psum(S3, "ps_ropeq", [P, TTW], F32)
                    qoffs = [reg[f"{lk}_wq"][GRP * g + j] for j in range(GRP)]
                    QS = b.dsem("qs")
                    w0 = [(PT, PT.n)]
                    for j in range(GRP):
                        b.dma(qslab[:, j * D:(j + 1) * D],
                              wf[:, qoffs[j]:qoffs[j] + D], waits=w0, sem=QS)
                        w0 = ()
                    qs_full = (QS, QS.n)
                    chain_mark = PT.n
                    for tt in range(NTT):
                        xn_ready, xn_marks = emit_xn(xn, tt, chain_mark)
                        pend_rope = []
                        for j in range(GRP):
                            bank = pq[j % 2]
                            evac_guard = PS.n - 1
                            for kc in range(KCD):
                                w = []
                                if j == 0:
                                    w.append((PV, xn_marks[kc]))
                                if kc == 0:
                                    w += [qs_full, (PS, evac_guard)]
                                    if j > 0:
                                        w.append((PV, xn_ready))
                                b.emit("tensor",
                                       lambda e, bank=bank, j=j, kc=kc: e.matmul(
                                           bank[:],
                                           qslab[:, j * D + kc * P:j * D + (kc + 1) * P],
                                           xn[:, kc * TTW:(kc + 1) * TTW],
                                           start=(kc == 0), stop=(kc == KCD - 1)),
                                       waits=w, incs=[(PT, 1)] if kc == KCD - 1 else ())
                            chain_mark = PT.n
                            dst = Q4[:, j * T + tt * TTW: j * T + (tt + 1) * TTW]
                            b.emit("scalar",
                                   lambda e, dst=dst, bank=bank: e.activation(
                                       dst, bank[:], AF.Copy, scale=kv[:, 2:3]),
                                   waits=[(PT, chain_mark)], incs=[(PS, 1)])
                            if lyr == 0:
                                pend_rope.append((dst, tt, PS.n))
                        for dst, rtt, eps_ in pend_rope:
                            emit_rope(dst, rtt, ps_rope, eps_)
                    b.flush()
                if dbg and g == 0:
                    b.tap(f"Q4_{lyr}", Q4[:])

                emit_attn_pipeline(lyr, g, K4, VT4, Q4, O4, S2)
                if dbg:
                    b.tap(f"O4_{lyr}_{g}", O4[:])

                with contextlib.ExitStack() as S3:
                    offs = reg[f"{lk}_wo"][16 * g:16 * (g + 1)]
                    ws = WStream(b, S3, "wostr", wf, offs, GRP * P)
                    wo_sb = b.sbuf(S3, "wo_sb", [P, 8 * TTW], F32)
                    ps_w = [b.psum(S3, f"ps_w{i}", [P, TTW], F32) for i in range(2)]
                    ws.start()
                    first = True
                    wo_sems = [b.dsem(f"wo{s}") for s in range(8)]
                    prev_stores = [(s, s.n) for s in wo_sems]
                    first_st = True
                    for oc in range(KCD):
                        slab, ldm = ws.use(oc)
                        for tt in range(NTT):
                            it = oc * NTT + tt
                            bank = ps_w[it % 2]
                            evac_guard = PS.n - 1
                            for j in range(GRP):
                                w = []
                                if j == 0:
                                    w = [ldm, (PS, evac_guard)]
                                    if first:
                                        w.append((PV, PV.n))
                                        first = False
                                b.emit("tensor",
                                       lambda e, bank=bank, slab=slab, j=j, tt=tt: e.matmul(
                                           bank[:], slab[:, j * P:(j + 1) * P],
                                           O4[:, j * T + tt * TTW: j * T + (tt + 1) * TTW],
                                           start=(j == 0), stop=(j == GRP - 1)),
                                       waits=w, incs=[(PT, 1)] if j == GRP - 1 else ())
                            slot = it % 8
                            wbuf = wo_sb[:, slot * TTW:(slot + 1) * TTW]
                            ssem = wo_sems[slot]
                            b.emit("scalar",
                                   lambda e, wbuf=wbuf, bank=bank: e.copy(wbuf, bank[:]),
                                   waits=[(PT, PT.n), (ssem, ssem.n)],
                                   incs=[(PS, 1)])
                            w = [(PS, PS.n)]
                            if first_st:
                                w += prev_stores
                                first_st = False
                            b.dma(xdel[oc * P:(oc + 1) * P, tt * TTW:(tt + 1) * TTW],
                                  wbuf, waits=w, accum=(g > 0), sem=ssem)
                        ws.done(oc)
                    b.wo_guard = [(s, s.n) for s in wo_sems]
                    b.flush()

        def emit_vtr(vj, j, tt, vm, vtmp, VT4, ptr, tr_marks):
            vsl = vtmp[:, (vj % 2) * TTW:(vj % 2 + 1) * TTW]
            for j2 in range(4):
                w = [(PV, PV.n)]
                if j2 == 0:
                    w.append((PS, vm))
                b.emit("tensor",
                       lambda e, j2=j2, vsl=vsl: e.transpose(
                           ptr[j2 % 2][:, 0:P], vsl[:, j2 * P:(j2 + 1) * P], ident[:]),
                       waits=w, incs=[(PT, 1)])
                dstv = VT4[:, j * T + (tt * 4 + j2) * P: j * T + (tt * 4 + j2 + 1) * P]
                b.emit("vector",
                       lambda e, dstv=dstv, j2=j2: e.tensor_copy(
                           dstv, ptr[j2 % 2][:, 0:P]),
                       waits=[(PT, PT.n)], incs=[(PV, 1)])
            tr_marks[vj] = PT.n

        def emit_attn_pipeline(lyr, g, K4, VT4, Q4, O4, S2):
            with contextlib.ExitStack() as S3:
                Srow2 = b.sbuf(S3, "Srow2", [P, 2 * SROW_W], F16)
                rcp32 = b.sbuf(S3, "rcp32", [1, TTW], F32)
                ps_s = [b.psum(S3, f"ps_s{i}", [P, TTW], F32) for i in range(2)]
                ps_tr = [b.psum(S3, f"ps_tr{i}", [P, 1024], F16) for i in range(2)]
                ps_o = [b.psum(S3, f"ps_o{i}", [P, TTW], F32) for i in range(2)]
                ps_sumf = b.psum(S3, "ps_sum", [P, TTW], F32)
                ps_sum1 = ps_sumf[0:1, :]
                ps_sum = [ps_sum1, ps_sum1]
                ps_nm = b.psum(S3, "ps_nm", [P, TTW], F32)

                iters = [(j, qt) for j in range(GRP) for qt in range(NTT)]
                NIT = getattr(b, "attn_nit", None) or len(iters)

                def meta(i):
                    j, qt = iters[i]
                    h = GRP * g + j
                    kidx = (h // 4) if lyr == 0 else j
                    slope = float(2.0 ** (-(h + 1) / 2.0)) if lyr == 0 else 1.0
                    return j, qt, kidx, slope, 4 * (qt + 1), i % 2

                st = dict(avm=[0, 0], stm=[0, 0], nmps=[0, 0], subpg=[0, 0],
                          sumfree=[0, 0], omul=[0, 0], rcpb_pv=0, rcpbc_pt=0,
                          stt_hist=[], exp_hist=[], sttm={})

                b.emit("tensor", lambda e: e.nop(),
                       waits=[(PS, PS.n), (PV, PV.n)], incs=[(PT, 1)])

                def phase1(i):
                    j, qt, kidx, slope, nkt, pi = meta(i)
                    h = GRP * g + j
                    di = (slopei[:, h * P:(h + 1) * P] if lyr == 0 else ident[:]) if BIASMM else None
                    srow = Srow2[:, pi * SROW_W:(pi + 1) * SROW_W]
                    mr = M_run[:, pi * TTW:(pi + 1) * TTW]
                    sttm = []
                    for kt in range(nkt):
                        bank = ps_s[kt % 2]
                        nglob = len(st["stt_hist"])
                        w = []
                        if nglob >= 2:
                            w.append((PV, st["stt_hist"][nglob - 2]))
                        j0 = 384 + qt * TTW - kt * P
                        if BIASMM:
                            b.emit("tensor",
                                   lambda e, bank=bank, j0=j0, di=di: e.matmul(
                                       bank[:], di, bt[:, j0:j0 + TTW],
                                       start=True, stop=False),
                                   waits=w)
                            b.emit("tensor",
                                   lambda e, bank=bank, kt=kt, kidx=kidx, j=j, qt=qt:
                                   e.matmul(
                                       bank[:],
                                       K4[:, kidx * T + kt * P: kidx * T + (kt + 1) * P],
                                       Q4[:, j * T + qt * TTW: j * T + (qt + 1) * TTW],
                                       start=False, stop=True),
                                   incs=[(PT, 1)])
                        else:
                            b.emit("tensor",
                                   lambda e, bank=bank, kt=kt, kidx=kidx, j=j, qt=qt:
                                   e.matmul(
                                       bank[:],
                                       K4[:, kidx * T + kt * P: kidx * T + (kt + 1) * P],
                                       Q4[:, j * T + qt * TTW: j * T + (qt + 1) * TTW],
                                       start=True, stop=True),
                                   waits=w, incs=[(PT, 1)])
                        mmark = PT.n
                        ssl = srow[:, kt * TTW:(kt + 1) * TTW]
                        w = [(PT, mmark)]
                        if kt == 0:
                            w.append((PT, st["avm"][pi]))
                        if BIASMM:
                            b.emit("vector",
                                   lambda e, ssl=ssl, bank=bank: e.tensor_copy(
                                       ssl, bank[:]),
                                   waits=w, incs=[(PV, 1)])
                        else:
                            b.emit("vector",
                                   lambda e, ssl=ssl, bank=bank, j0=j0, slope=slope:
                                   e.scalar_tensor_tensor(
                                       ssl, bt[:, j0:j0 + TTW], slope, bank[:],
                                       ALU.mult, ALU.add),
                                   waits=w, incs=[(PV, 1)])
                        sttm.append(PV.n)
                        st["stt_hist"].append(PV.n)
                        w = [(PV, sttm[kt])]
                        if kt == 0:
                            w.append((PT, st["stm"][pi]))
                            b.emit("vector",
                                   lambda e, ssl=ssl, mr=mr: e.tensor_copy(mr, ssl),
                                   waits=w)
                        else:
                            b.emit("vector",
                                   lambda e, ssl=ssl, mr=mr: e.tensor_max(mr, mr, ssl),
                                   waits=w, incs=[(PV, 1)] if kt == nkt - 1 else ())
                    st["sttm"][i] = sttm
                    return PV.n

                def stats(i, runmax_pv):
                    j, qt, kidx, slope, nkt, pi = meta(i)
                    mr = M_run[:, pi * TTW:(pi + 1) * TTW]
                    mxs = mxcol[:, pi * 4:(pi + 1) * 4]
                    mq = M_q[0:1, pi * TTW:(pi + 1) * TTW]
                    nmx = negmax[:, pi * TTW:(pi + 1) * TTW]
                    for j2 in range(4):
                        w = [(PV, PV.n)]
                        if j2 == 0:
                            w.append((PV, runmax_pv))
                        b.emit("tensor",
                               lambda e, j2=j2, mr=mr: e.transpose(
                                   ps_tr[j2 % 2][:, 0:P], mr[:, j2 * P:(j2 + 1) * P],
                                   ident[:]),
                               waits=w, incs=[(PT, 1)])
                        w = [(PT, PT.n)]
                        if j2 == 0:
                            w.append((PT, st["stm"][pi]))
                        b.emit("vector",
                               lambda e, j2=j2, mxs=mxs: e.reduce_max(
                                   mxs[:, j2:j2 + 1], ps_tr[j2 % 2][0:P, 0:P], axis=AX.X),
                               waits=w, incs=[(PV, 1)])
                    for j2 in range(4):
                        b.emit("tensor",
                               lambda e, j2=j2, mxs=mxs: e.transpose(
                                   ps_tr[j2 % 2][0:1, 0:P], mxs[:, j2:j2 + 1], ident[:]),
                               waits=[(PV, PV.n)], incs=[(PT, 1)])
                        b.emit("vector",
                               lambda e, j2=j2, mq=mq: e.tensor_scalar_mul(
                                   mq[0:1, j2 * P:(j2 + 1) * P], ps_tr[j2 % 2][0:1, 0:P],
                                   -1.0),
                               waits=[(PT, PT.n)], incs=[(PV, 1)])
                    b.emit("tensor",
                           lambda e, mq=mq: e.matmul(ps_nm[:], ones_r[:], mq,
                                                     start=True, stop=True),
                           waits=[(PV, PV.n), (PV, st["rcpb_pv"])], incs=[(PT, 1)])
                    b.emit("scalar",
                           lambda e, nmx=nmx: e.copy(nmx, ps_nm[:]),
                           waits=[(PT, PT.n), (PV, st["subpg"][pi])], incs=[(PS, 1)])
                    st["nmps"][pi] = PS.n
                    st["stm"][pi] = PT.n

                def phase2(i):
                    j, qt, kidx, slope, nkt, pi = meta(i)
                    srow = Srow2[:, pi * SROW_W:(pi + 1) * SROW_W]
                    nmx = negmax[:, pi * TTW:(pi + 1) * TTW]
                    for kt in range(nkt):
                        ssl = srow[:, kt * TTW:(kt + 1) * TTW]
                        tsl = tmpsub[:, (kt % 2) * TTW:(kt % 2 + 1) * TTW]
                        nexp = len(st["exp_hist"])
                        w = []
                        if kt == 0:
                            w.append((PS, st["nmps"][pi]))
                        if nexp >= 2:
                            w.append((PS, st["exp_hist"][nexp - 2]))
                        b.emit("vector",
                               lambda e, ssl=ssl, tsl=tsl, nmx=nmx: e.tensor_add(
                                   tsl, ssl, nmx),
                               waits=w, incs=[(PV, 1)])
                        b.emit("scalar",
                               lambda e, ssl=ssl, tsl=tsl: e.activation(ssl, tsl, AF.Exp),
                               waits=[(PV, PV.n)], incs=[(PS, 1)])
                        expm = PS.n
                        st["exp_hist"].append(expm)
                        w = [(PS, expm)]
                        if kt == 0:
                            if getattr(b, "old_recip", False):
                                w.append((PV, st["rcpb_pv"]))
                            else:
                                w.append((PS, max(st["sumfree"])))
                        b.emit("tensor",
                               lambda e, ssl=ssl, kt=kt, pi=pi: e.matmul(
                                   ps_sum[pi], ones_c[:], ssl,
                                   start=(kt == 0), stop=(kt == nkt - 1),
                                   skip_group_check=True),
                               waits=w)
                        w = []
                        if kt == 0:
                            w.append((PV, st["omul"][pi]))
                        b.emit("tensor",
                               lambda e, ssl=ssl, kt=kt, pi=pi, kidx=kidx: e.matmul(
                                   ps_o[pi][:],
                                   VT4[:, kidx * T + kt * P: kidx * T + (kt + 1) * P],
                                   ssl, start=(kt == 0), stop=(kt == nkt - 1),
                                   skip_group_check=True),
                               waits=w, incs=[(PT, 1)] if kt == nkt - 1 else ())
                    st["avm"][pi] = PT.n
                    st["subpg"][pi] = PV.n
                    if getattr(b, "old_recip", False):
                        b.emit("vector",
                               lambda e, pi=pi: e.reciprocal(rcp32[0:1, :], ps_sum[pi]),
                               waits=[(PT, st["avm"][pi]), (PT, st["rcpbc_pt"])],
                               incs=[(PV, 1)])
                        st["sumfree"][pi] = PS.n
                        b.emit("tensor",
                               lambda e: e.matmul(ps_nm[:], ones_r32[:], rcp32[0:1, :],
                                                  start=True, stop=True),
                               waits=[(PV, PV.n), (PS, st["nmps"][pi])], incs=[(PT, 1)])
                    else:
                        b.emit("scalar",
                               lambda e, pi=pi: e.activation(lnr[0:1, :], ps_sum[pi],
                                                             AF.Ln),
                               waits=[(PT, st["avm"][pi])], incs=[(PS, 1)])
                        st["sumfree"][pi] = PS.n
                        b.emit("scalar",
                               lambda e: e.activation(rcp_r[0:1, :], lnr[0:1, :], AF.Exp,
                                                      scale=kv[0:1, 3:4]),
                               waits=[(PT, st["rcpbc_pt"])], incs=[(PS, 1)])
                        b.emit("tensor",
                               lambda e: e.matmul(ps_nm[:], ones_r[:], rcp_r[0:1, :],
                                                  start=True, stop=True),
                               waits=[(PS, PS.n), (PS, st["nmps"][pi])], incs=[(PT, 1)])
                    st["rcpbc_pt"] = PT.n
                    b.emit("vector",
                           lambda e: e.tensor_copy(rcp_b[:], ps_nm[:]),
                           waits=[(PT, st["rcpbc_pt"])], incs=[(PV, 1)])
                    st["rcpb_pv"] = PV.n
                    osl = O4[:, j * T + qt * TTW: j * T + (qt + 1) * TTW]
                    b.emit("vector",
                           lambda e, osl=osl, pi=pi: e.tensor_mul(
                               osl, ps_o[pi][:], rcp_b[:]),
                           incs=[(PV, 1)])
                    st["omul"][pi] = PV.n

                pg = phase1(0)
                stats(0, pg)
                for i in range(1, NIT):
                    pg = phase1(i)
                    phase2(i - 1)
                    stats(i, pg)
                phase2(NIT - 1)
                if b.dbg and g == getattr(b, "tap_group", 1):
                    b.tap(f"Srow2_{lyr}", Srow2[:])
                    b.tap(f"negmax_{lyr}", negmax[:])
                    b.tap(f"Mq_{lyr}", M_q[:])
                    b.tap(f"rcpb_{lyr}", rcp_b[:])
                    b.tap(f"Mrun_{lyr}", M_run[:])
                    b.tap(f"mxcol_{lyr}", mxcol[:])
                    b.tap(f"lnr_{lyr}", lnr[:])
                    b.tap(f"rcpr_{lyr}", rcp_r[:])
                b.flush()

        def emit_ffn(lyr):
            lk = "l0" if lyr == 0 else "l1"
            quant = (lyr == 1)
            src = wi if quant else wf
            sg_o, _ = sv_reg[f"{lk}_gs"]
            su_o, _ = sv_reg[f"{lk}_us"]
            sd_o, _ = sv_reg[f"{lk}_ds"]
            with contextlib.ExitStack() as S2:
                xn = b.sbuf(S2, "xnf", [P, KCD * TTW], F16)
                hbuf = b.sbuf(S2, "hbuf", [P, NIC * TTW], F16)
                xd = b.sbuf(S2, "xd", [P, KCD * TTW], F16)
                ps_g = [b.psum(S2, f"fg{i}", [P, TTW], F32) for i in range(2)]
                ps_u = [b.psum(S2, f"fu{i}", [P, TTW], F32) for i in range(2)]
                ps_d = [b.psum(S2, f"fd{i}", [P, TTW], F32) for i in range(2)]
                ps_r = b.psum(S2, "fr", [P, TTW], F32)
                XD = b.dsem("xd")
                chain_mark = PT.n
                xd_guard = 0
                for tt in range(NTT):
                    xn_ready, xn_marks = emit_xn(xn, tt, chain_mark)
                    for kc in range(KCD):
                        w = []
                        if kc == 0:
                            w = list(b.wo_guard) + [(PV, xd_guard), (PT, chain_mark)]
                        b.dma(xd[:, kc * TTW:(kc + 1) * TTW],
                              xdel[kc * P:(kc + 1) * P, tt * TTW:(tt + 1) * TTW],
                              waits=w, sem=XD)
                    xd_full = (XD, XD.n)
                    if quant:
                        for kc in range(KCD):
                            w = [(PV, xn_marks[kc])]
                            if kc == 0:
                                w.append((PS, PS.n))
                            b.emit("tensor",
                                   lambda e, kc=kc: e.matmul(
                                       ps_r[0:1, :], ones_c[:],
                                       xn[:, kc * TTW:(kc + 1) * TTW],
                                       start=(kc == 0), stop=(kc == KCD - 1),
                                       skip_group_check=True),
                                   waits=w, incs=[(PT, 1)] if kc == KCD - 1 else ())
                        b.emit("scalar",
                               lambda e: e.copy(row16[0:1, 0:TTW], ps_r[0:1, :]),
                               waits=[(PT, PT.n)], incs=[(PS, 1)])
                        b.emit("tensor",
                               lambda e: e.matmul(ps_r[:], ones_r[:],
                                                  row16[0:1, 0:TTW],
                                                  start=True, stop=True),
                               waits=[(PS, PS.n)], incs=[(PT, 1)])
                        b.emit("scalar", lambda e: e.copy(r1sb[:], ps_r[:]),
                               waits=[(PT, PT.n)], incs=[(PS, 1)])
                    with contextlib.ExitStack() as S3:
                        gs = WStream(b, S3, "gstr", src, reg[f"{lk}_gate"], D)
                        us = WStream(b, S3, "ustr", src, reg[f"{lk}_up"], D)
                        gs.start()
                        us.start()
                        first_chain = True
                        for ic in range(NIC):
                            for which in (0, 1):
                                stream = gs if which == 0 else us
                                bank = (ps_g if which == 0 else ps_u)[ic % 2]
                                slab, ldm = stream.use(ic)
                                if not quant and which == 0:
                                    guard = (PS, PS.n - 1)
                                else:
                                    guard = (PV, PV.n - 1)
                                for kc in range(KCD):
                                    w = []
                                    if first_chain:
                                        w.append((PV, xn_marks[kc]))
                                    if kc == 0:
                                        w += [ldm, guard]
                                        if not first_chain:
                                            w.append((PV, xn_ready))
                                    b.emit("tensor",
                                           lambda e, bank=bank, slab=slab, kc=kc: e.matmul(
                                               bank[:], slab[:, kc * P:(kc + 1) * P],
                                               xn[:, kc * TTW:(kc + 1) * TTW],
                                               start=(kc == 0), stop=(kc == KCD - 1)),
                                           waits=w,
                                           incs=[(PT, 1)] if kc == KCD - 1 else ())
                                first_chain = False
                                cmk = PT.n
                                stream.done(ic)
                                sgap = sv_sb[:, sg_o + ic:sg_o + ic + 1]
                                suap = sv_sb[:, su_o + ic:su_o + ic + 1]
                                hsl = hbuf[:, ic * TTW:(ic + 1) * TTW]
                                if which == 0:
                                    if quant:
                                        gt = gtmp[:, 0:TTW]
                                        b.emit("vector",
                                               lambda e, gt=gt, bank=bank:
                                               e.scalar_tensor_tensor(
                                                   gt, r1sb[:], -128.0, bank[:],
                                                   ALU.mult, ALU.add),
                                               waits=[(PT, cmk), (PS, PS.n)],
                                               incs=[(PV, 1)])
                                        b.emit("scalar",
                                               lambda e, gt=gt, sgap=sgap: e.activation(
                                                   gbuf[:], gt, AF.Silu, scale=sgap),
                                               waits=[(PV, PV.n)], incs=[(PS, 1)])
                                    else:
                                        b.emit("scalar",
                                               lambda e, bank=bank, sgap=sgap: e.activation(
                                                   gbuf[:], bank[:], AF.Silu, scale=sgap),
                                               waits=[(PT, cmk), sv_mark,
                                                      (PV, PV.n)],
                                               incs=[(PS, 1)])
                                else:
                                    if quant:
                                        ut = gtmp[:, TTW:2 * TTW]
                                        b.emit("vector",
                                               lambda e, ut=ut, bank=bank:
                                               e.scalar_tensor_tensor(
                                                   ut, r1sb[:], -128.0, bank[:],
                                                   ALU.mult, ALU.add),
                                               waits=[(PT, cmk)])
                                        b.emit("vector",
                                               lambda e, ut=ut, hsl=hsl, suap=suap:
                                               e.scalar_tensor_tensor(
                                                   hsl, ut, suap, gbuf[:],
                                                   ALU.mult, ALU.mult),
                                               waits=[(PS, PS.n)], incs=[(PV, 1)])
                                    else:
                                        b.emit("vector",
                                               lambda e, bank=bank, hsl=hsl, suap=suap:
                                               e.scalar_tensor_tensor(
                                                   hsl, bank[:], suap, gbuf[:],
                                                   ALU.mult, ALU.mult),
                                               waits=[(PT, cmk), (PS, PS.n)],
                                               incs=[(PV, 1)])
                        b.flush()
                    hdone_pv = PV.n
                    if dbg:
                        b.tap(f"xd_{lyr}_{tt}", xd[:])
                        b.tap(f"hb_{lyr}_{tt}", hbuf[:])
                    if quant:
                        for ic in range(NIC):
                            w = [(PV, hdone_pv), (PS, PS.n)] if ic == 0 else []
                            b.emit("tensor",
                                   lambda e, ic=ic: e.matmul(
                                       ps_r[0:1, :], ones_c[:],
                                       hbuf[:, ic * TTW:(ic + 1) * TTW],
                                       start=(ic == 0), stop=(ic == NIC - 1),
                                       skip_group_check=True),
                                   waits=w, incs=[(PT, 1)] if ic == NIC - 1 else ())
                        b.emit("scalar",
                               lambda e: e.copy(row16[0:1, TTW:2 * TTW], ps_r[0:1, :]),
                               waits=[(PT, PT.n)], incs=[(PS, 1)])
                        b.emit("tensor",
                               lambda e: e.matmul(ps_r[:], ones_r[:],
                                                  row16[0:1, TTW:2 * TTW],
                                                  start=True, stop=True),
                               waits=[(PS, PS.n)], incs=[(PT, 1)])
                        b.emit("scalar", lambda e: e.copy(rhsb[:], ps_r[:]),
                               waits=[(PT, PT.n)], incs=[(PS, 1)])
                        rh_ps = PS.n
                    with contextlib.ExitStack() as S3:
                        ds = WStream(b, S3, "dstr", src, reg[f"{lk}_down"], INTER)
                        ds.start()
                        for oc in range(KCD):
                            bank = ps_d[oc % 2]
                            guard = PV.n - 1
                            slab, ldm = ds.use(oc)
                            for ic in range(NIC):
                                w = []
                                if ic == 0:
                                    w = [ldm, (PV, hdone_pv), (PV, guard)]
                                b.emit("tensor",
                                       lambda e, bank=bank, slab=slab, ic=ic: e.matmul(
                                           bank[:], slab[:, ic * P:(ic + 1) * P],
                                           hbuf[:, ic * TTW:(ic + 1) * TTW],
                                           start=(ic == 0), stop=(ic == NIC - 1)),
                                       waits=w,
                                       incs=[(PT, 1)] if ic == NIC - 1 else ())
                            ds.done(oc)
                            cmk = PT.n
                            sdap = sv_sb[:, sd_o + oc:sd_o + oc + 1]
                            xsl = X[oc][:, tt * TTW:(tt + 1) * TTW]
                            xdsl = xd[:, oc * TTW:(oc + 1) * TTW]
                            gt = gtmp[:, 0:TTW]
                            if quant:
                                b.emit("vector",
                                       lambda e, gt=gt, bank=bank: e.scalar_tensor_tensor(
                                           gt, rhsb[:], -128.0, bank[:],
                                           ALU.mult, ALU.add),
                                       waits=[(PT, cmk), (PS, rh_ps)])
                                b.emit("vector",
                                       lambda e, gt=gt, sdap=sdap, xdsl=xdsl:
                                       e.scalar_tensor_tensor(
                                           gt, gt, sdap, xdsl, ALU.mult, ALU.add),
                                       waits=[xd_full])
                                b.emit("vector",
                                       lambda e, gt=gt, xsl=xsl: e.tensor_add(
                                           xsl, gt, xsl),
                                       incs=[(PV, 1)])
                            else:
                                b.emit("vector",
                                       lambda e, bank=bank, gt=gt, sdap=sdap, xdsl=xdsl:
                                       e.scalar_tensor_tensor(
                                           gt, bank[:], sdap, xdsl, ALU.mult, ALU.add),
                                       waits=[(PT, cmk), xd_full])
                                b.emit("vector",
                                       lambda e, gt=gt, xsl=xsl: e.tensor_add(
                                           xsl, gt, xsl),
                                       incs=[(PV, 1)])
                        b.flush()
                    if dbg:
                        b.tap(f"hbA_{lyr}_{tt}", hbuf[:])
                        b.tap(f"xdA_{lyr}_{tt}", xd[:])
                        b.tap(f"XA_{lyr}_{tt}", X[0][:])
                    chain_mark = PT.n
                    xd_guard = PV.n
                b.flush()

        for lyr in layers:
            o, w = cst_reg["bt0" if lyr == 0 else "bt1"]
            b.dma(bt[:], cst[:, o:o + w], waits=[(PV, PV.n)], sem=b.dsem("cs"))
            b.cst_mark = (b.dsem("cs"), b.dsem("cs").n)
            emit_layernorm(lyr)
            if dbg:
                b.tap(f"mb{lyr}", m_b[:])
                b.tap(f"rb{lyr}", r_b[:])
            emit_attention(lyr)
            emit_ffn(lyr)
            if dbg:
                for c in range(KCD):
                    b.tap(f"X{lyr}_{c}", X[c][:])

        final_pv = PV.n
        FIN = b.dsem("fin")
        for c in range(KCD):
            b.dma(y_t[c * P:(c + 1) * P, :], X[c][:], waits=[(PV, final_pv)], sem=FIN)
        endw = [(FIN, FIN.n)]
        if dbg:
            endw.append((b.dsem("dbg"), b.dsem("dbg").n))
        b.emit("gpsimd", lambda e: e.nop(), waits=endw)
        b.flush()
    return nc



def kernel(**inputs):
    hp0 = prepare(inputs)
    hp = dict(reg=hp0["reg"], sv_reg=hp0["sv_reg"], cst_reg=hp0["cst_reg"],
              wf_w=hp0["wf"].shape[1], wi_w=hp0["wi"].shape[1],
              sv_w=hp0["sv"].shape[1], cst_w=hp0["cst"].shape[1])
    nc = build_program(hp, layers=(0, 1))
    x = np.asarray(inputs["L0_attn_x"])
    in_maps = []
    for core in range(8):
        xt = np.ascontiguousarray(x[core].T)
        in_maps.append(dict(x_t=xt, wf=hp0["wf"], wi=hp0["wi"],
                            sv=hp0["sv"], cst=hp0["cst"]))
    res = run_bass_kernel_spmd(nc, in_maps, core_ids=list(range(8)))
    out = np.empty((8, T, D), dtype=np.float32)
    for core in range(8):
        out[core] = np.asarray(res.results[core]["y_t"]).T
    return out
